# version 18
# baseline (speedup 1.0000x reference)
import os
import numpy as np
from contextlib import ExitStack

import concourse.bass as bass
import concourse.mybir as mybir
from concourse.bass_utils import run_bass_kernel_spmd
from concourse.library_overlay import lower_extended_insts

F32 = mybir.dt.float32
BF16 = mybir.dt.bfloat16
AF = mybir.ActivationFunctionType
ALU = mybir.AluOpType
AX = mybir.AxisListType

N, E, H, D = 2048, 1024, 16, 64
NT = N // 128
FI = E // 128
NCORES = 8
EPS = 1e-5
M_FFN, INNER = 64, 16


class Prog:
    ENGS = ("t", "v", "s", "y", "g")
    QUEUES = ("y", "g")
    NLANES = 4

    def __init__(self):
        self.stream = {e: [] for e in self.ENGS}
        self.sems = ["t", "v", "s"] + [f"{q}{i}" for q in self.QUEUES for i in range(self.NLANES)]
        self.cnt = {s_: 0 for s_ in self.sems}
        self.lastwait = {e: {} for e in self.ENGS}
        self.qlane = {q: 0 for q in self.QUEUES}
        self.writer = {}
        self.readers = {}
        self.stream_of_sem = {}
        for e in ("t", "v", "s"):
            self.stream_of_sem[e] = e
        for q in self.QUEUES:
            for i in range(self.NLANES):
                self.stream_of_sem[f"{q}{i}"] = q

    def _need(self, waits, sem, val):
        if val > waits.get(sem, 0):
            waits[sem] = val

    def emit(self, eng, fn, reads=(), writes=(), inc=1, trows=(0, 128)):
        is_q = eng in self.QUEUES
        if is_q:
            lane = self.qlane[eng]
            self.qlane[eng] = (lane + 1) % self.NLANES
            mysem = f"{eng}{lane}"
        else:
            mysem = eng
        waits = {}
        if eng == "t":
            for b in writes:
                if b.startswith("pb"):
                    prev = getattr(self, "psum_rows", {}).get(b)
                    if prev is not None and (trows[1] <= prev[0] or prev[1] <= trows[0]):
                        self._need(waits, "t", prev[2])
            if not hasattr(self, "psum_rows"):
                self.psum_rows = {}
            for b in writes:
                if b.startswith("pb"):
                    self.psum_rows[b] = (trows[0], trows[1], self.cnt["t"] + 1)
        if is_q and self.cnt[mysem] > 0:
            self._need(waits, mysem, self.cnt[mysem])
        for b in reads:
            w = self.writer.get(b)
            if w is not None:
                ws, wc = w
                same_stream = self.stream_of_sem[ws] == eng
                if (not same_stream) or is_q or self.stream_of_sem[ws] in ("v", "s", "g"):
                    self._need(waits, ws, wc)
        for b in writes:
            for rs, rc in self.readers.get(b, {}).items():
                if self.stream_of_sem[rs] != eng or is_q or eng in ("v", "s", "g"):
                    self._need(waits, rs, rc)
            w = self.writer.get(b)
            if w is not None:
                ws, wc = w
                same_stream = self.stream_of_sem[ws] == eng
                if (not same_stream) or is_q or (eng in ("v", "s") and same_stream):
                    self._need(waits, ws, wc)
        for sem, val in waits.items():
            if val > self.lastwait[eng].get(sem, 0):
                self.stream[eng].append(("wait", sem, val))
                self.lastwait[eng][sem] = val
        self.stream[eng].append(("op", fn, inc, mysem))
        self.cnt[mysem] += inc
        c = self.cnt[mysem]
        for b in reads:
            self.readers.setdefault(b, {})[mysem] = c
        for b in writes:
            self.writer[b] = (mysem, c)
            self.readers[b] = {}

    def barrier(self):
        for e in self.ENGS:
            for sem in self.sems:
                own = (self.stream_of_sem[sem] == e) and e not in self.QUEUES
                if not own and self.cnt[sem] > self.lastwait[e].get(sem, 0):
                    self.stream[e].append(("wait", sem, self.cnt[sem]))
                    self.lastwait[e][sem] = self.cnt[sem]

    def final_waits(self):
        for q in self.QUEUES:
            for i in range(self.NLANES):
                sem = f"{q}{i}"
                if self.cnt[sem] > self.lastwait[q].get(sem, 0):
                    self.stream[q].append(("wait", sem, self.cnt[sem]))
                    self.lastwait[q][sem] = self.cnt[sem]


def build_nc():
    PH = int(os.environ.get('KPH', '4'))
    nc = bass.Bass("TRN2", target_bir_lowering=False)

    dt_in = lambda n_, s_: nc.dram_tensor(n_, s_, F32, kind="ExternalInput").ap()
    x_h = dt_in("x", [N, E])
    wq_h = dt_in("wq", [128, 8192])
    wk_h = dt_in("wk", [128, 8192])
    wv_h = dt_in("wv", [128, 8192])
    wo_h = dt_in("wo", [128, 8192])
    bqc_h = dt_in("bqc", [128, 8])
    bkc_h = dt_in("bkc", [128, 8])
    bvr_h = dt_in("bvr", [128, 1024])
    win_h = dt_in("win", [128, 1024])
    binp_h = dt_in("binp", [128, 1])
    w1p_h = dt_in("w1p", [128, 8])
    b1p_h = dt_in("b1p", [128, 8])
    w2s_h = dt_in("w2s", [128, 512])
    b2v_h = dt_in("b2v", [64, 1])
    wout_h = dt_in("wout", [64, 1024])
    bor_h = dt_in("bor", [128, 1024])
    boutr_h = dt_in("boutr", [128, 1024])
    mask0_h = dt_in("mask0", [128, 192])
    maskm_h = dt_in("maskm", [128, 256])
    mask15_h = dt_in("mask15", [128, 192])
    ident_h = dt_in("identc", [128, 128])
    ones_h = dt_in("onesc", [128, 1])
    y_h = nc.dram_tensor("y", [N, E], F32, kind="ExternalOutput").ap()

    es = ExitStack()
    sbuf = lambda n_, s_, d_: es.enter_context(nc.sbuf_tensor(n_, s_, d_))
    psum = lambda n_, s_, d_: es.enter_context(nc.psum_tensor(n_, s_, d_))

    qf = sbuf("qf", [128, 16384], BF16)
    kf = sbuf("kf", [128, 16384], BF16)
    vtsh = sbuf("vtsh", [128, 16 * 1024], BF16)
    wbig = sbuf("wbig", [128, 8192], BF16)
    ident = sbuf("ident", [128, 128], BF16)
    ones_b = sbuf("ones_b", [128, 1], BF16)
    wout_sb = sbuf("wout_sb", [64, 1024], BF16)
    bqc = sbuf("bqc_s", [128, 8], F32)
    bkc = sbuf("bkc_s", [128, 8], F32)
    binp = sbuf("binp_s", [128, 1], F32)
    w1p = sbuf("w1p_s", [128, 8], F32)
    b1p = sbuf("b1p_s", [128, 8], F32)
    b2v = sbuf("b2v_s", [64, 1], F32)
    g_sb = sbuf("g_sb", [64, 2048], BF16)
    lns = sbuf("lns", [128, 16], F32)

    pb = [psum(f"pb{i}", [128, 512], F32) for i in range(8)]
    pbf = [b.ap().bitcast(BF16) for b in pb]

    sems = {}
    P = Prog()

    ENG = {}

    def op(eng, fn, reads=(), writes=(), inc=1, trows=(0, 128)):
        P.emit(eng, fn, reads=reads, writes=writes, inc=inc, trows=trows)

    op("g", lambda: ENG["g"].dma_start(ident[:, :], ident_h), writes=["ident"], inc=16)
    op("g", lambda: ENG["g"].dma_start(ones_b[:, :], ones_h), writes=["ones_b"], inc=16)
    op("g", lambda: ENG["g"].dma_start(win_sb[:, :], win_h), writes=["win_sb"], inc=16)
    op("g", lambda: ENG["g"].dma_start(w2s_sb[:, :], w2s_h), writes=["w2s_sb"], inc=16)
    op("g", lambda: ENG["g"].dma_start(wout_sb[:, :], wout_h), writes=["wout_sb"], inc=16)
    op("y", lambda: ENG["y"].dma_start(bqc[:, :], bqc_h), writes=["bqc"], inc=16)
    op("y", lambda: ENG["y"].dma_start(bkc[:, :], bkc_h), writes=["bkc"], inc=16)
    op("y", lambda: ENG["y"].dma_start(binp[:, :], binp_h), writes=["binp"], inc=16)
    op("y", lambda: ENG["y"].dma_start(w1p[:, :], w1p_h), writes=["w1p"], inc=16)
    op("y", lambda: ENG["y"].dma_start(b1p[:, :], b1p_h), writes=["b1p"], inc=16)
    op("y", lambda: ENG["y"].dma_start(b2v[:, :], b2v_h), writes=["b2v"], inc=16)

    esX = ExitStack()
    xnt = esX.enter_context(nc.sbuf_tensor("xnt", [128, 16384], BF16))
    esA = ExitStack()
    sbufA = lambda n_, s_, d_: esA.enter_context(nc.sbuf_tensor(n_, s_, d_))
    win_sb = sbufA("win_sb", [128, 1024], BF16)
    w2s_sb = sbufA("w2s_sb", [128, 512], BF16)
    xin = sbufA("xin", [128, 4096], F32)
    xnrow = sbufA("xnrow", [128, 2048], BF16)
    s2 = sbufA("s2", [128, 2048], BF16)
    hb = sbufA("hb", [128, 2048], BF16)
    bvr = sbufA("bvr_s", [128, 1024], BF16)
    wv_sb = sbufA("wv_sb", [128, 8192], BF16)


    op("g", lambda: ENG["g"].dma_start(bvr[:, :], bvr_h), writes=["bvr"], inc=16)
    for i in range(4):
        op("g", lambda i=i: ENG["g"].dma_start(wv_sb[:, i * 2048:(i + 1) * 2048],
                                               wv_h[:, i * 2048:(i + 1) * 2048]),
           writes=[f"wv{i}"], inc=16)

    def xslab_ap(base):
        return x_h.rearrange("(a p) e -> p a e", p=128)[:, base:base + 2, :] if False else \
            bass.AP(x_h.tensor, base * 128 * 1024,
                    [[1024, 128], [128 * 1024, 2], [1, 1024]])

    def p1_a(t):
        c0 = (t % 2) * 8
        slab, sl2 = t // 2, (t // 2) % 2
        if t % 2 == 0:
            op("y", lambda slab=slab, sl2=sl2: ENG["y"].dma_start(
                xin[:, :].rearrange("p (a e) -> p a e", a=4)[:, sl2 * 2:sl2 * 2 + 2, :],
                xslab_ap(slab * 2)),
               writes=[f"xsl{sl2}"], inc=16)
        xoff = (sl2 * 2 + (t % 2)) * 1024
        op("s", lambda xoff=xoff, c0=c0: ENG["s"].activation(
            hb[:, 0:1024], xin[:, xoff:xoff + 1024], AF.Square,
            accum_out=lns[:, c0 + 1:c0 + 2]),
           reads=[f"xsl{sl2}"], writes=["hb0", f"lns_s2_{t % 2}"])
        op("v", lambda xoff=xoff, c0=c0: ENG["v"].reduce_sum(
            lns[:, c0 + 0:c0 + 1], xin[:, xoff:xoff + 1024], axis=AX.X),
           reads=[f"xsl{sl2}"], writes=[f"lns_s1_{t % 2}"])

    def p1_b(t):
        sl, c0 = t % 2, (t % 2) * 8
        op("v", lambda c0=c0: ENG["v"].tensor_scalar(
            lns[:, c0 + 2:c0 + 4], lns[:, c0 + 0:c0 + 2], 1.0 / 1024.0, None, op0=ALU.mult),
           reads=[f"lns_s1_{sl}", f"lns_s2_{sl}"], writes=[f"lns_m_{sl}", f"lns_e_{sl}"])
        op("v", lambda c0=c0: ENG["v"].tensor_tensor(
            lns[:, c0 + 4:c0 + 5], lns[:, c0 + 2:c0 + 3], lns[:, c0 + 2:c0 + 3], op=ALU.mult),
           reads=[f"lns_m_{sl}"], writes=[f"lns_m2_{sl}"])
        op("v", lambda c0=c0: ENG["v"].tensor_scalar(
            lns[:, c0 + 5:c0 + 6], lns[:, c0 + 3:c0 + 4], None, EPS,
            op0=ALU.subtract, op1=ALU.add) if False else ENG["v"].tensor_tensor(
            lns[:, c0 + 5:c0 + 6], lns[:, c0 + 3:c0 + 4], lns[:, c0 + 4:c0 + 5], op=ALU.subtract),
           reads=[f"lns_e_{sl}", f"lns_m2_{sl}"], writes=[f"lns_v_{sl}"])
        op("v", lambda c0=c0: ENG["v"].tensor_scalar(
            lns[:, c0 + 5:c0 + 6], lns[:, c0 + 5:c0 + 6], EPS, None, op0=ALU.add),
           reads=[f"lns_v_{sl}"], writes=[f"lns_v_{sl}"])
        op("v", lambda c0=c0: ENG["v"].reciprocal(
            lns[:, c0 + 6:c0 + 7], lns[:, c0 + 5:c0 + 6]),
           reads=[f"lns_v_{sl}"], writes=[f"lns_rv_{sl}"])
        op("s", lambda c0=c0: ENG["s"].activation(
            lns[:, c0 + 6:c0 + 7], lns[:, c0 + 6:c0 + 7], AF.Sqrt),
           reads=[f"lns_rv_{sl}"], writes=[f"lns_r_{sl}"])
        op("v", lambda c0=c0: ENG["v"].tensor_tensor(
            lns[:, c0 + 7:c0 + 8], lns[:, c0 + 2:c0 + 3], lns[:, c0 + 6:c0 + 7], op=ALU.mult),
           reads=[f"lns_m_{sl}", f"lns_r_{sl}"], writes=[f"lns_mr_{sl}"])
        op("v", lambda c0=c0: ENG["v"].tensor_scalar(
            lns[:, c0 + 7:c0 + 8], lns[:, c0 + 7:c0 + 8], -1.0, None, op0=ALU.mult),
           reads=[f"lns_mr_{sl}"], writes=[f"lns_nmr_{sl}"])

    def p1_c(t):
        sl, c0 = t % 2, (t % 2) * 8
        sl2 = (t // 2) % 2
        xoff = (sl2 * 2 + (t % 2)) * 1024
        op("s", lambda xoff=xoff, sl=sl, c0=c0: ENG["s"].activation(
            xnrow[:, sl * 1024:(sl + 1) * 1024], xin[:, xoff:xoff + 1024],
            AF.Identity, bias=lns[:, c0 + 7:c0 + 8], scale=lns[:, c0 + 6:c0 + 7]),
           reads=[f"xsl{sl2}", f"lns_r_{sl}", f"lns_nmr_{sl}"], writes=[f"xnrow{sl}"])
        bank = 6 + sl
        for fi in range(FI):
            op("t", lambda fi=fi, sl=sl, bank=bank: ENG["t"].transpose(
                pbf[bank][:, fi * 128:(fi + 1) * 128],
                xnrow[:, sl * 1024 + fi * 128: sl * 1024 + (fi + 1) * 128], ident[:, :]),
               reads=[f"xnrow{sl}", "ident"], writes=[f"pb{bank}"])
        def evac_xnt(t=t, bank=bank):
            dst = xnt[:, :].rearrange("p (c n) -> p c n", c=FI)[:, :, t * 128:(t + 1) * 128]
            srcp = pbf[bank][:, 0:1024].rearrange("p (c n) -> p c n", c=FI)
            return ENG["v"].tensor_copy(dst, srcp)
        op("v", evac_xnt, reads=[f"pb{bank}"], writes=[f"xnt{t}"])

    def vproj(j):
        tok0 = 128 * j - 64
        pstart = 64 if j == 0 else 0
        pcnt = 64 if j in (0, 16) else 128
        a0 = max(tok0, 0)
        tiles = sorted({a0 // 128, (a0 + pcnt - 1) // 128})
        xr = [f"xnt{tt}" for tt in tiles]
        bank0 = (j % 2) * 2
        for half in range(2):
            bank = bank0 + half
            for fi in range(FI):
                op("t", lambda j=j, half=half, fi=fi, bank=bank, a0=a0, pcnt=pcnt:
                   ENG["t"].matmul(
                       pb[bank][0:pcnt, :],
                       xnt[:, fi * 2048 + a0: fi * 2048 + a0 + pcnt],
                       wv_sb[:, fi * 1024 + half * 512: fi * 1024 + (half + 1) * 512],
                       start=(fi == 0), stop=(fi == FI - 1)),
                   reads=xr + [f"wv{fi // 2}"], writes=[f"pb{bank}"])
            jc = 0 if j == 16 else j
            op("v", lambda jc=jc, half=half, bank=bank, pstart=pstart, pcnt=pcnt, j=j:
               ENG["v"].tensor_tensor(
                   vtsh[pstart:pstart + pcnt, jc * 1024 + half * 512: jc * 1024 + (half + 1) * 512],
                   pb[bank][0:pcnt, :],
                   bvr[pstart:pstart + pcnt, half * 512:(half + 1) * 512], op=ALU.add),
               reads=[f"pb{bank}", "bvr"], writes=[f"vtsh{j}h{half}"])

    for t in range(NT + 6):
        if t < NT:
            p1_a(t)
        if 0 <= t - 1 < NT:
            p1_b(t - 1)
        if 0 <= t - 2 < NT:
            p1_c(t - 2)
        if PH >= 2 and 0 <= t - 3 <= 16:
            vproj(t - 3)

    if PH >= 2:
        for pi, (wh, dst, bias, nm) in enumerate(
                [(wq_h, qf, bqc, "q")]):
            for fo in range(FI):
                slot = (pi * 8 + fo) % 8
                op("g", lambda wh=wh, fo=fo, slot=slot: ENG["g"].dma_start(
                    wbig[:, slot * 1024:(slot + 1) * 1024], wh[:, fo * 1024:(fo + 1) * 1024]),
                   writes=[f"wslot{slot}"], inc=16)
                banks = [4 + (fo % 2) * 2, 5 + (fo % 2) * 2]
                for tc in range(4):
                    bank = banks[tc % 2]
                    bname = f"pb{bank}"
                    for fi in range(FI):
                        op("t", lambda fo=fo, tc=tc, fi=fi, slot=slot, bank=bank:
                           ENG["t"].matmul(
                               pb[bank][:, :],
                               wbig[:, slot * 1024 + fi * 128: slot * 1024 + (fi + 1) * 128],
                               xnt[:, fi * 2048 + tc * 512: fi * 2048 + (tc + 1) * 512],
                               start=(fi == 0), stop=(fi == FI - 1)),
                           reads=[f"wslot{slot}"] + [f"xnt{tt}" for tt in range(tc * 4, tc * 4 + 4)],
                           writes=[bname])
                    op("s", lambda dst=dst, bias=bias, fo=fo, tc=tc, bank=bank:
                       ENG["s"].activation(
                           dst[:, fo * 2048 + tc * 512: fo * 2048 + (tc + 1) * 512],
                           pb[bank][:, :], AF.Identity, bias=bias[:, fo:fo + 1], scale=1.0),
                       reads=[f"pb{bank}", "bqc" if nm == "q" else "bkc"],
                       writes=[f"{nm}f{fo}_{tc}"])

        for fo in range(FI):
            op("g", lambda fo=fo: ENG["g"].dma_start(
                wbig[:, fo * 1024:(fo + 1) * 1024], wk_h[:, fo * 1024:(fo + 1) * 1024]),
               writes=[f"wslot{fo}"], inc=16)

        for tc in range(4):
            bank = 0 + (tc % 2)
            for fi in range(FI):
                op("t", lambda tc=tc, fi=fi, bank=bank: ENG["t"].matmul(
                    pb[bank][:, :],
                    win_sb[:, fi * 128:(fi + 1) * 128],
                    xnt[:, fi * 2048 + tc * 512: fi * 2048 + (tc + 1) * 512],
                    start=(fi == 0), stop=(fi == FI - 1)),
                   reads=["win_sb"] + [f"xnt{tt}" for tt in range(tc * 4, tc * 4 + 4)],
                   writes=[f"pb{bank}"])
            op("s", lambda tc=tc, bank=bank: ENG["s"].activation(
                s2[:, tc * 512:(tc + 1) * 512], pb[bank][:, :], AF.Identity,
                bias=binp[:, :], scale=1.0),
               reads=[f"pb{bank}", "binp"], writes=[f"s2_{tc // 2}"])
        for jj in range(8):
            for hf in range(2):
                sl = (jj * 2 + hf) % 2
                op("s", lambda jj=jj, hf=hf, sl=sl: ENG["s"].activation(
                    hb[:, sl * 1024:(sl + 1) * 1024], s2[:, hf * 1024:(hf + 1) * 1024], AF.Relu,
                    bias=b1p[:, jj:jj + 1], scale=w1p[:, jj:jj + 1]),
                   reads=[f"s2_{hf}", "w1p", "b1p"], writes=[f"hb{sl}"])
                for q2 in range(2):
                    bank = 2 + (hf * 2 + q2) % 4
                    op("t", lambda jj=jj, hf=hf, sl=sl, q2=q2, bank=bank: ENG["t"].matmul(
                        pb[bank][0:64, :],
                        w2s_sb[:, jj * 64:(jj + 1) * 64],
                        hb[:, sl * 1024 + q2 * 512: sl * 1024 + (q2 + 1) * 512],
                        start=(jj == 0), stop=(jj == 7)),
                       reads=[f"hb{sl}", "w2s_sb"], writes=[f"pb{bank}"])
        for tc in range(4):
            bank = 2 + tc
            op("s", lambda tc=tc, bank=bank: ENG["s"].activation(
                g_sb[:, tc * 512:(tc + 1) * 512], pb[bank][0:64, :], AF.Identity,
                bias=b2v[:, :], scale=1.0),
               reads=[f"pb{bank}", "b2v"], writes=["g"])

    P.barrier()
    esA.close()

    esBkeep = ExitStack()
    afm = esBkeep.enter_context(nc.sbuf_tensor("afm", [128, 16384], BF16))
    esB = ExitStack()
    sbufB = lambda n_, s_, d_: esB.enter_context(nc.sbuf_tensor(n_, s_, d_))
    mms = sbufB("mms", [128, 256], BF16)
    m0s = mms[:, 64:256]
    m15s = mms[:, 0:192]
    svdgsv = sbufB("svdgsv", [33, 1024], BF16)
    svdg = svdgsv[0:4, :]
    svrow = svdgsv[32:33, :]
    emrt = sbufB("emrt", [4, 256], BF16)
    stt = sbufB("stt", [128, 112], F32)
    emrec = sbufB("emrec", [128, 16], BF16)
    wbuf = sbufB("wbuf", [128, 2048], F32)
    ebuf = sbufB("ebuf", [128, 2048], BF16)
    ppb = sbufB("ppb", [128, 2048], BF16)
    ppt = sbufB("ppt", [128, 2048], BF16)

    op("g", lambda: ENG["g"].dma_start(mms[:, :], maskm_h), writes=["mm", "m0", "m15"], inc=16)


    if PH >= 3:
        for half in range(2):
            for j in range(16):
                op("t", lambda j=j, half=half: ENG["t"].matmul(
                    pb[6 + half][0:1, 0:512],
                    ones_b[:, :],
                    vtsh[:, j * 1024 + half * 512: j * 1024 + (half + 1) * 512],
                    start=(j == 0), stop=(j == 15)),
                   reads=[f"vtsh{j}h{half}", "ones_b"] +
                         ([f"vtsh16h{half}"] if j == 0 else []),
                   writes=[f"pb{6 + half}"])
        for half in range(2):
            op("v", lambda half=half: ENG["v"].tensor_copy(
                svrow[:, half * 512:(half + 1) * 512], pb[6 + half][0:1, :]),
               reads=[f"pb{6 + half}"], writes=["svrow"])
        op("v", lambda: ENG["v"].memset(svdg[:, :], 0.0), writes=["svdg"])
        for h in range(H):
            g_, r_ = h // 4, h % 4
            op("g", lambda h=h, g_=g_, r_=r_: ENG["g"].dma_start(
                svdg[r_:r_ + 1, g_ * 256 + r_ * 64: g_ * 256 + (r_ + 1) * 64],
                svrow[0:1, h * 64:(h + 1) * 64]),
               reads=["svrow"], writes=["svdg"], inc=16)

        def qt_params(qt):
            if qt == 0:
                return 192, "m0", m0s, 0, [(0, 64, 0, 64), (64, 128, 1, 0)]
            if qt == NT - 1:
                return 192, "m15", m15s, 1856, [(0, 128, 15, 0), (128, 64, 16, 0)]
            return 256, "mm", mms, 128 * qt - 64, [(0, 128, qt, 0), (128, 128, qt + 1, 0)]

        NTASK = NT * 4

        def stage_scores(k):
            qt, g_ = divmod(k, 4)
            W, mname, mask, wstart, pieces = qt_params(qt)
            pair = (0, 1) if k % 2 == 0 else (2, 3)
            for j in range(4):
                h = 4 * g_ + j
                hc, hbase = h // 2, (h % 2) * 64
                bank = pair[j % 2]
                col = (j // 2) * 256
                op("t", lambda h=h, hc=hc, hbase=hbase, qt=qt, W=W, wstart=wstart,
                       bank=bank, col=col:
                   ENG["t"].matmul(
                       pb[bank][:, col:col + W],
                       qf[hbase:hbase + 64, hc * 2048 + qt * 128: hc * 2048 + (qt + 1) * 128],
                       kf[hbase:hbase + 64, hc * 2048 + wstart: hc * 2048 + wstart + W],
                       start=True, stop=True, tile_position=(hbase, 0)),
                   reads=[f"qf{hc}_{qt // 4}"] +
                         [f"kf{hc}_{tc}" for tc in range(wstart // 512, (wstart + W - 1) // 512 + 1)],
                   writes=[f"pb{bank}"], trows=(hbase, hbase + 64))

        def stage_sm_a(k):
            qt, g_ = divmod(k, 4)
            W, mname, mask, wstart, pieces = qt_params(qt)
            pair = (0, 1) if k % 2 == 0 else (2, 3)
            c0 = 4 * g_
            for j in range(4):
                h = c0 + j
                bank, col = pair[j % 2], (j // 2) * 256
                op("v", lambda h=h, bank=bank, col=col, W=W, mask=mask:
                   ENG["v"].tensor_tensor_reduce(
                       wbuf[:, (h % 8) * 256: (h % 8) * 256 + W],
                       pb[bank][:, col:col + W],
                       mask[:, 0:W], 1.0, 0.0, ALU.mult, ALU.max,
                       accum_out=stt[:, h:h + 1]),
                   reads=[f"pb{bank}", mname], writes=[f"wS{h % 8}", f"mb{h}"])
            op("s", lambda c0=c0: ENG["s"].mul(
                stt[:, 16 + c0:16 + c0 + 4], stt[:, c0:c0 + 4], -1.0),
               reads=[f"mb{h}" for h in range(c0, c0 + 4)], writes=[f"negmG{g_}"])

        def stage_sm_b(k):
            qt, g_ = divmod(k, 4)
            W, mname, mask, wstart, pieces = qt_params(qt)
            c0 = 4 * g_
            op("s", lambda c0=c0: ENG["s"].activation(
                stt[:, 48 + c0:48 + c0 + 4], stt[:, c0:c0 + 4], AF.Exp, scale=-1.0),
               reads=[f"mb{h}" for h in range(c0, c0 + 4)], writes=[f"emG{g_}"])
            for j in range(4):
                h = c0 + j
                op("s", lambda h=h, W=W: ENG["s"].activation(
                    ebuf[:, (h % 8) * 256: (h % 8) * 256 + W], wbuf[:, (h % 8) * 256: (h % 8) * 256 + W], AF.Exp,
                    bias=stt[:, 16 + h:16 + h + 1], scale=1.0,
                    accum_out=stt[:, 32 + h:32 + h + 1]),
                   reads=[f"wS{h % 8}", f"negmG{g_}"], writes=[f"eS{h % 8}", f"sp{h}"])

        def stage_sm_c(k):
            qt, g_ = divmod(k, 4)
            W, mname, mask, wstart, pieces = qt_params(qt)
            c0 = 4 * g_
            op("v", lambda c0=c0, W=W: ENG["v"].tensor_scalar(
                stt[:, 64 + c0:64 + c0 + 4], stt[:, 48 + c0:48 + c0 + 4],
                float(N - W), None, op0=ALU.mult),
               reads=[f"emG{g_}"], writes=[f"ztG{g_}"])
            op("v", lambda c0=c0: ENG["v"].tensor_tensor(
                stt[:, 80 + c0:80 + c0 + 4], stt[:, 64 + c0:64 + c0 + 4],
                stt[:, 32 + c0:32 + c0 + 4], op=ALU.add),
               reads=[f"ztG{g_}"] + [f"sp{h}" for h in range(c0, c0 + 4)],
               writes=[f"zG{g_}"])
            op("v", lambda c0=c0: ENG["v"].reciprocal(
                stt[:, 96 + c0:96 + c0 + 4], stt[:, 80 + c0:80 + c0 + 4]),
               reads=[f"zG{g_}"], writes=[f"rzG{g_}"])
            op("v", lambda c0=c0: ENG["v"].tensor_tensor(
                emrec[:, c0:c0 + 4], stt[:, 48 + c0:48 + c0 + 4],
                stt[:, 96 + c0:96 + c0 + 4], op=ALU.mult),
               reads=[f"emG{g_}", f"rzG{g_}"], writes=[f"emrG{g_}"])
            for j in range(4):
                h = c0 + j
                op("v", lambda h=h, W=W: ENG["v"].tensor_scalar(
                    ppb[:, (h % 8) * 256: (h % 8) * 256 + W], ebuf[:, (h % 8) * 256: (h % 8) * 256 + W],
                    stt[:, 48 + h:48 + h + 1], stt[:, 96 + h:96 + h + 1],
                    op0=ALU.subtract, op1=ALU.mult),
                   reads=[f"eS{h % 8}", f"emG{g_}", f"rzG{g_}"], writes=[f"ppS{h % 8}"])

        def stage_pt(k):
            qt, g_ = divmod(k, 4)
            W, mname, mask, wstart, pieces = qt_params(qt)
            bank = 4
            ev = "v" if qt % 2 == 0 else "s"
            c0 = 4 * g_
            op("t", lambda c0=c0, bank=bank: ENG["t"].transpose(
                pbf[bank][0:4, 0:128], emrec[:, c0:c0 + 4], ident[:, :]),
               reads=[f"emrG{g_}", "ident"], writes=[f"pb{bank}"])
            op(ev, lambda bank=bank, ev=ev, k=k: (ENG["v"].tensor_copy if ev == "v" else ENG["s"].copy)(
                emrt[0:4, (k % 2) * 128:(k % 2) * 128 + 128], pbf[bank][0:4, 0:128]),
               reads=[f"pb{bank}"], writes=[f"emrt{k % 2}"])
            if qt in (0, NT - 1):
                for j in range(4):
                    if qt == 0:
                        zap = pb[bank][0:64, j * 128: j * 128 + 64]
                    else:
                        zap = pb[bank][64:128, j * 128 + 64: j * 128 + 128]
                    op("v", lambda zap=zap: ENG["v"].memset(zap, 0.0),
                       reads=[], writes=[f"pb{bank}"])
            for j in range(4):
                h = c0 + j
                for i, (cc0, cw, vj, pstart) in enumerate(pieces):
                    op("t", lambda h=h, j=j, i=i, cc0=cc0, cw=cw, pstart=pstart, bank=bank:
                       ENG["t"].transpose(
                           pbf[bank][pstart:pstart + cw, j * 256 + i * 128: j * 256 + (i + 1) * 128],
                           ppb[:, (h % 8) * 256 + cc0: (h % 8) * 256 + cc0 + cw],
                           ident[:, :], tile_position=(0, pstart)),
                       reads=[f"ppS{h % 8}", "ident"], writes=[f"pb{bank}"])
            op(ev, lambda bank=bank, k=k, ev=ev: (ENG["v"].tensor_copy if ev == "v" else ENG["s"].copy)(
                ppt[:, (k % 2) * 1024:(k % 2) * 1024 + 1024], pbf[bank][:, 0:1024]),
               reads=[f"pb{bank}"], writes=[f"ppts{k % 2}"])

        def stage_pv(k):
            qt, g_ = divmod(k, 4)
            W, mname, mask, wstart, pieces = qt_params(qt)
            c0 = 4 * g_
            ev = "v" if qt % 2 == 0 else "s"
            for j in range(4):
                h = c0 + j
                cch = 2 * g_ + j // 2
                bankp = 6 + (j % 2)
                op("t", lambda h=h, g_=g_, j=j, bankp=bankp, k=k: ENG["t"].matmul(
                    pb[bankp][0:64, 0:128],
                    svdg[0:4, g_ * 256 + j * 64: g_ * 256 + (j + 1) * 64],
                    emrt[0:4, (k % 2) * 128:(k % 2) * 128 + 128],
                    start=True, stop=False, tile_position=(0, 0)),
                   reads=["svdg", f"emrt{k % 2}"], writes=[f"pb{bankp}"], trows=(0, 4))
                for i, (cc0, cw, vj, pstart) in enumerate(pieces):
                    op("t", lambda h=h, j=j, i=i, cw=cw, vj=vj, pstart=pstart,
                           bankp=bankp, k=k:
                       ENG["t"].matmul(
                           pb[bankp][0:64, 0:128],
                           vtsh[pstart:pstart + cw,
                                (0 if vj == 16 else vj) * 1024 + h * 64:
                                (0 if vj == 16 else vj) * 1024 + (h + 1) * 64],
                           ppt[pstart:pstart + cw,
                               (k % 2) * 1024 + j * 256 + i * 128: (k % 2) * 1024 + j * 256 + (i + 1) * 128],
                           start=False, stop=(i == 1), tile_position=(pstart, 0)),
                       reads=[f"ppts{k % 2}", f"vtsh{vj}h{h // 8}"], writes=[f"pb{bankp}"],
                       trows=(pstart, pstart + cw))
                op(ev, lambda qt=qt, h=h, cch=cch, bankp=bankp, ev=ev:
                   (ENG["v"].tensor_copy if ev == "v" else ENG["s"].copy)(
                       afm[(h % 2) * 64:(h % 2) * 64 + 64,
                           cch * 2048 + qt * 128: cch * 2048 + (qt + 1) * 128],
                       pb[bankp][0:64, 0:128]),
                   reads=[f"pb{bankp}"], writes=[f"afm{qt}_{g_ // 2}"])

        def kproj_group(fo, tc):
            for fi in range(FI):
                op("t", lambda fo=fo, tc=tc, fi=fi: ENG["t"].matmul(
                    pb[5][:, :],
                    wbig[:, fo * 1024 + fi * 128: fo * 1024 + (fi + 1) * 128],
                    xnt[:, fi * 2048 + tc * 512: fi * 2048 + (tc + 1) * 512],
                    start=(fi == 0), stop=(fi == FI - 1)),
                   reads=[f"wslot{fo}"] + [f"xnt{tt}" for tt in range(tc * 4, tc * 4 + 4)],
                   writes=["pb5"])
            op("s", lambda fo=fo, tc=tc: ENG["s"].activation(
                kf[:, fo * 2048 + tc * 512: fo * 2048 + (tc + 1) * 512],
                pb[5][:, :], AF.Identity, bias=bkc[:, fo:fo + 1], scale=1.0),
               reads=["pb5", "bkc"], writes=[f"kf{fo}_{tc}"])

        for fo in range(FI):
            kproj_group(fo, 0)

        for k in range(NTASK + 5):
            if k < NTASK:
                stage_scores(k)
            if 0 <= k - 1 < NTASK:
                stage_sm_a(k - 1)
            if 0 <= k - 2 < NTASK:
                stage_sm_b(k - 2)
            if 0 <= k - 3 < NTASK:
                stage_sm_c(k - 3)
            if 0 <= k - 4 < NTASK:
                stage_pt(k - 4)
            if 0 <= k - 5 < NTASK:
                stage_pv(k - 5)
            if k % 2 == 0 and k < 48:
                kproj_group((k % 16) // 2, 1 + k // 16)
            if k >= 48 and k < 56 and k % 2 == 0:
                i = (k - 48) // 2
                op("g", lambda i=i: ENG["g"].dma_start(
                    wbig[:, i * 2048:(i + 1) * 2048], wo_h[:, i * 2048:(i + 1) * 2048]),
                   writes=[f"wslot{2 * i}", f"wslot{2 * i + 1}"], inc=16)

    P.barrier()
    esB.close()
    qfF = qf[:, :].bitcast(F32)
    kfF = kf[:, :].bitcast(F32)
    xin4 = qfF[:, 0:4096]
    xb = qfF[:, 4096:7168]
    outs = kfF[:, 0:2048]
    bcomb = kfF[:, 2048:3072]

    op("y", lambda: ENG["y"].dma_start(outs[:, 0:1024], bor_h), writes=["outs0"], inc=16)
    op("y", lambda: ENG["y"].dma_start(outs[:, 1024:2048], boutr_h), writes=["outs1"], inc=16)
    op("v", lambda: ENG["v"].tensor_tensor(bcomb[:, :], outs[:, 0:1024], outs[:, 1024:2048],
                                           op=ALU.add),
       reads=["outs0", "outs1"], writes=["bcomb"])

    if PH >= 4:
        def p4_a(t):
            slab, sl2 = t // 2, (t // 2) % 2
            if t % 2 == 0:
                op("g", lambda slab=slab, sl2=sl2: ENG["g"].dma_start(
                    xin4[:, :].rearrange("p (a e) -> p a e", a=4)[:, sl2 * 2:sl2 * 2 + 2, :],
                    xslab_ap(slab * 2)),
                   writes=[f"xsl{sl2}"], inc=16)
            xoff = (sl2 * 2 + (t % 2)) * 1024
            op("v", lambda t=t, xoff=xoff, sl2=sl2: ENG["v"].tensor_tensor(
                xb[:, (t % 3) * 1024:(t % 3 + 1) * 1024],
                xin4[:, xoff:xoff + 1024], bcomb[:, :], op=ALU.add),
               reads=[f"xsl{sl2}", "bcomb"], writes=[f"xb{t % 3}"])

        def p4_b(t):
            for hf in range(2):
                bank = (t % 2) * 2 + hf
                for c in range(FI):
                    op("t", lambda t=t, hf=hf, c=c, bank=bank: ENG["t"].matmul(
                        pb[bank][:, :],
                        afm[:, c * 2048 + t * 128: c * 2048 + (t + 1) * 128],
                        wbig[:, c * 1024 + hf * 512: c * 1024 + (hf + 1) * 512],
                        start=(c == 0), stop=False),
                       reads=[f"afm{t}_{c // 4}", f"wslot{c}"], writes=[f"pb{bank}"])
                op("t", lambda t=t, hf=hf, bank=bank: ENG["t"].matmul(
                    pb[bank][:, :],
                    g_sb[:, t * 128:(t + 1) * 128],
                    wout_sb[:, hf * 512:(hf + 1) * 512],
                    start=False, stop=True),
                   reads=["g", "wout_sb"], writes=[f"pb{bank}"], trows=(0, 64))

        def p4_c(t):
            sl = t % 2
            for hf in range(2):
                bank = (t % 2) * 2 + hf
                op("v", lambda t=t, sl=sl, hf=hf, bank=bank: ENG["v"].tensor_tensor(
                    outs[:, sl * 1024 + hf * 512: sl * 1024 + (hf + 1) * 512],
                    pb[bank][:, :], xb[:, (t % 3) * 1024 + hf * 512: (t % 3) * 1024 + (hf + 1) * 512],
                    op=ALU.add),
                   reads=[f"pb{bank}", f"xb{t % 3}"], writes=[f"outs{sl}"])
            op("y", lambda t=t, sl=sl: ENG["y"].dma_start(
                y_h[t * 128:(t + 1) * 128, :], outs[:, sl * 1024:(sl + 1) * 1024]),
               reads=[f"outs{sl}"], writes=[], inc=16)

        for t in range(NT + 2):
            if 0 <= t - 2 < NT:
                p4_c(t - 2)
            if t < NT:
                p4_a(t)
            if 0 <= t - 1 < NT:
                p4_b(t - 1)

    P.final_waits()

    semh = {}
    for s_ in P.sems:
        semh[s_] = es.enter_context(nc.semaphore(f"sem_{s_}"))
    block = es.enter_context(nc.Block())

    def replay(e, engobj):
        ENG[e] = engobj
        for item in P.stream[e]:
            if item[0] == "wait":
                engobj.wait_ge(semh[item[1]], item[2])
            else:
                instr = item[1]()
                instr.then_inc(semh[item[3]], item[2])

    @block.tensor
    def _(t):
        replay("t", t)

    @block.vector
    def _(v):
        replay("v", v)

    @block.scalar
    def _(s):
        replay("s", s)

    @block.sync
    def _(s):
        replay("y", s)

    @block.gpsimd
    def _(g):
        replay("g", g)

    esBkeep.close()
    esX.close()
    es.close()
    lower_extended_insts(nc)
    return nc


def _host_consts(i):
    f = np.ascontiguousarray
    Wq = np.asarray(i["L0_attn_Wq"], np.float32)
    Wk = np.asarray(i["L0_attn_Wk"], np.float32)
    Wv = np.asarray(i["L0_attn_Wv"], np.float32)
    Wo = np.asarray(i["L0_attn_Wo"], np.float32)
    bq = np.asarray(i["L0_attn_bq"], np.float32)
    bk = np.asarray(i["L0_attn_bk"], np.float32)
    bv = np.asarray(i["L0_attn_bv"], np.float32)
    bo = np.asarray(i["L0_attn_bo"], np.float32)
    Win = np.asarray(i["L0_ffn_Win"], np.float32)
    bin_ = np.asarray(i["L0_ffn_bin_"], np.float32)
    W1 = np.asarray(i["L0_ffn_W1"], np.float32)
    b1 = np.asarray(i["L0_ffn_b1"], np.float32)
    W2 = np.asarray(i["L0_ffn_W2"], np.float32)
    b2 = np.asarray(i["L0_ffn_b2"], np.float32)
    Wout = np.asarray(i["L0_ffn_Wout"], np.float32)
    bout = np.asarray(i["L0_ffn_bout"], np.float32)

    d = {}
    d["wq"] = f(Wq.reshape(8, 128, 8, 128).transpose(1, 2, 0, 3).reshape(128, 8192))
    d["wk"] = f(Wk.reshape(8, 128, 8, 128).transpose(1, 2, 0, 3).reshape(128, 8192))
    d["wv"] = f(Wv.reshape(8, 128, 1024).transpose(1, 0, 2).reshape(128, 8192))
    d["wo"] = f(Wo.reshape(8, 128, 1024).transpose(1, 0, 2).reshape(128, 8192))
    d["bqc"] = f(bq.reshape(8, 128).T)
    d["bkc"] = f(bk.reshape(8, 128).T)
    d["bvr"] = f(np.broadcast_to(bv[None, :], (128, 1024)))
    wind = Win.reshape(8, 128, 64)
    d["win"] = f(np.concatenate([wind, wind], axis=2).transpose(1, 0, 2).reshape(128, 1024))
    d["binp"] = f(np.concatenate([bin_, bin_]).reshape(128, 1))
    d["w1p"] = f(np.concatenate([W1[:, 0::2], W1[:, 1::2]], axis=0))
    d["b1p"] = f(np.concatenate([b1[:, 0::2], b1[:, 1::2]], axis=0))
    w2s = np.zeros((128, 512), np.float32)
    for m in range(64):
        for j in range(8):
            w2s[m, j * 64 + m] = W2[m, 2 * j]
            w2s[64 + m, j * 64 + m] = W2[m, 2 * j + 1]
    d["w2s"] = w2s
    d["b2v"] = f(b2.reshape(64, 1))
    d["wout"] = f(Wout)
    d["bor"] = f(np.broadcast_to(bo[None, :], (128, 1024)))
    d["boutr"] = f(np.broadcast_to(bout[None, :], (128, 1024)))
    r = np.arange(128)[:, None]
    c0 = np.arange(192)[None, :]
    cm = np.arange(256)[None, :]
    d["mask0"] = (np.abs(c0 - r) <= 64).astype(np.float32)
    d["maskm"] = ((cm >= r) & (cm <= r + 128)).astype(np.float32)
    d["mask15"] = ((c0 >= r) & (c0 <= r + 128)).astype(np.float32)
    d["identc"] = np.eye(128, dtype=np.float32)
    d["onesc"] = np.ones((128, 1), np.float32)
    return d


_NC_CACHE = None


def kernel(**inputs):
    global _NC_CACHE
    x = np.ascontiguousarray(np.asarray(inputs["L0_attn_x"], np.float32))
    consts = _host_consts(inputs)
    if _NC_CACHE is None:
        _NC_CACHE = build_nc()
    nc = _NC_CACHE
    in_maps = []
    for c in range(NCORES):
        m = {"x": np.ascontiguousarray(x[c])}
        m.update(consts)
        in_maps.append(m)
    res = run_bass_kernel_spmd(nc, in_maps, core_ids=list(range(NCORES)))
    return np.stack([res.results[c]["y"] for c in range(NCORES)], axis=0)


if __name__ == "__main__":
    import reference as _r
    ins = _r.setup_inputs()
    ins = {k: np.asarray(v) for k, v in ins.items()}
    out = kernel(**ins)
    ref = np.asarray(_r.reference(**{k: v for k, v in ins.items()}))
    err = np.abs(out - ref)
    rv = float(((out - ref) ** 2).mean() / (ref ** 2).mean())
    print("max abs err:", err.max(), "resid var ratio:", rv)


# version 19
# speedup vs baseline: 1.0194x; 1.0194x over previous
import os
import numpy as np
from contextlib import ExitStack

import concourse.bass as bass
import concourse.mybir as mybir
from concourse.bass_utils import run_bass_kernel_spmd
from concourse.library_overlay import lower_extended_insts

F32 = mybir.dt.float32
BF16 = mybir.dt.bfloat16
AF = mybir.ActivationFunctionType
ALU = mybir.AluOpType
AX = mybir.AxisListType

N, E, H, D = 2048, 1024, 16, 64
NT = N // 128
FI = E // 128
NCORES = 8
EPS = 1e-5
M_FFN, INNER = 64, 16


class Prog:
    ENGS = ("t", "v", "s", "y", "g")
    QUEUES = ("y", "g")
    NLANES = 4

    def __init__(self):
        self.stream = {e: [] for e in self.ENGS}
        self.sems = ["t", "v", "s"] + [f"{q}{i}" for q in self.QUEUES for i in range(self.NLANES)]
        self.cnt = {s_: 0 for s_ in self.sems}
        self.lastwait = {e: {} for e in self.ENGS}
        self.qlane = {q: 0 for q in self.QUEUES}
        self.writer = {}
        self.readers = {}
        self.stream_of_sem = {}
        for e in ("t", "v", "s"):
            self.stream_of_sem[e] = e
        for q in self.QUEUES:
            for i in range(self.NLANES):
                self.stream_of_sem[f"{q}{i}"] = q

    def _need(self, waits, sem, val):
        if val > waits.get(sem, 0):
            waits[sem] = val

    def emit(self, eng, fn, reads=(), writes=(), inc=1, trows=(0, 128)):
        is_q = eng in self.QUEUES
        if is_q:
            lane = self.qlane[eng]
            self.qlane[eng] = (lane + 1) % self.NLANES
            mysem = f"{eng}{lane}"
        else:
            mysem = eng
        waits = {}
        if eng == "t":
            for b in writes:
                if b.startswith("pb"):
                    prev = getattr(self, "psum_rows", {}).get(b)
                    if prev is not None and (trows[1] <= prev[0] or prev[1] <= trows[0]):
                        self._need(waits, "t", prev[2])
            if not hasattr(self, "psum_rows"):
                self.psum_rows = {}
            for b in writes:
                if b.startswith("pb"):
                    self.psum_rows[b] = (trows[0], trows[1], self.cnt["t"] + 1)
        if is_q and self.cnt[mysem] > 0:
            self._need(waits, mysem, self.cnt[mysem])
        for b in reads:
            w = self.writer.get(b)
            if w is not None:
                ws, wc = w
                same_stream = self.stream_of_sem[ws] == eng
                if (not same_stream) or is_q or self.stream_of_sem[ws] in ("v", "s", "g"):
                    self._need(waits, ws, wc)
        for b in writes:
            for rs, rc in self.readers.get(b, {}).items():
                if self.stream_of_sem[rs] != eng or is_q or eng in ("v", "s", "g"):
                    self._need(waits, rs, rc)
            w = self.writer.get(b)
            if w is not None:
                ws, wc = w
                same_stream = self.stream_of_sem[ws] == eng
                if (not same_stream) or is_q or (eng in ("v", "s") and same_stream):
                    self._need(waits, ws, wc)
        for sem, val in waits.items():
            if val > self.lastwait[eng].get(sem, 0):
                self.stream[eng].append(("wait", sem, val))
                self.lastwait[eng][sem] = val
        self.stream[eng].append(("op", fn, inc, mysem))
        self.cnt[mysem] += inc
        c = self.cnt[mysem]
        for b in reads:
            self.readers.setdefault(b, {})[mysem] = c
        for b in writes:
            self.writer[b] = (mysem, c)
            self.readers[b] = {}

    def barrier(self):
        for e in self.ENGS:
            for sem in self.sems:
                own = (self.stream_of_sem[sem] == e) and e not in self.QUEUES
                if not own and self.cnt[sem] > self.lastwait[e].get(sem, 0):
                    self.stream[e].append(("wait", sem, self.cnt[sem]))
                    self.lastwait[e][sem] = self.cnt[sem]

    def final_waits(self):
        for q in self.QUEUES:
            for i in range(self.NLANES):
                sem = f"{q}{i}"
                if self.cnt[sem] > self.lastwait[q].get(sem, 0):
                    self.stream[q].append(("wait", sem, self.cnt[sem]))
                    self.lastwait[q][sem] = self.cnt[sem]


def build_nc():
    PH = int(os.environ.get('KPH', '4'))
    nc = bass.Bass("TRN2", target_bir_lowering=False)

    dt_in = lambda n_, s_: nc.dram_tensor(n_, s_, F32, kind="ExternalInput").ap()
    x_h = dt_in("x", [N, E])
    wq_h = dt_in("wq", [128, 8192])
    wk_h = dt_in("wk", [128, 8192])
    wv_h = dt_in("wv", [128, 8192])
    wo_h = dt_in("wo", [128, 8192])
    bqc_h = dt_in("bqc", [128, 8])
    bkc_h = dt_in("bkc", [128, 8])
    bvr_h = dt_in("bvr", [128, 1024])
    win_h = dt_in("win", [128, 1024])
    binp_h = dt_in("binp", [128, 1])
    w1p_h = dt_in("w1p", [128, 8])
    b1p_h = dt_in("b1p", [128, 8])
    w2s_h = dt_in("w2s", [128, 512])
    b2v_h = dt_in("b2v", [64, 1])
    wout_h = dt_in("wout", [64, 1024])
    bor_h = dt_in("bor", [128, 1024])
    boutr_h = dt_in("boutr", [128, 1024])
    mask0_h = dt_in("mask0", [128, 192])
    maskm_h = dt_in("maskm", [128, 256])
    mask15_h = dt_in("mask15", [128, 192])
    ident_h = dt_in("identc", [128, 128])
    ones_h = dt_in("onesc", [128, 1])
    y_h = nc.dram_tensor("y", [N, E], F32, kind="ExternalOutput").ap()

    es = ExitStack()
    sbuf = lambda n_, s_, d_: es.enter_context(nc.sbuf_tensor(n_, s_, d_))
    psum = lambda n_, s_, d_: es.enter_context(nc.psum_tensor(n_, s_, d_))

    qf = sbuf("qf", [128, 16384], BF16)
    kf = sbuf("kf", [128, 16384], BF16)
    vtsh = sbuf("vtsh", [128, 16 * 1024], BF16)
    wbig = sbuf("wbig", [128, 8192], BF16)
    ident = sbuf("ident", [128, 128], BF16)
    ones_b = sbuf("ones_b", [128, 1], BF16)
    wout_sb = sbuf("wout_sb", [64, 1024], BF16)
    bqc = sbuf("bqc_s", [128, 8], F32)
    bkc = sbuf("bkc_s", [128, 8], F32)
    binp = sbuf("binp_s", [128, 1], F32)
    w1p = sbuf("w1p_s", [128, 8], F32)
    b1p = sbuf("b1p_s", [128, 8], F32)
    b2v = sbuf("b2v_s", [64, 1], F32)
    g_sb = sbuf("g_sb", [64, 2048], BF16)
    lns = sbuf("lns", [128, 16], F32)

    pb = [psum(f"pb{i}", [128, 512], F32) for i in range(8)]
    pbf = [b.ap().bitcast(BF16) for b in pb]

    sems = {}
    P = Prog()

    ENG = {}

    def op(eng, fn, reads=(), writes=(), inc=1, trows=(0, 128)):
        P.emit(eng, fn, reads=reads, writes=writes, inc=inc, trows=trows)

    op("g", lambda: ENG["g"].dma_start(ident[:, :], ident_h), writes=["ident"], inc=16)
    op("g", lambda: ENG["g"].dma_start(ones_b[:, :], ones_h), writes=["ones_b"], inc=16)
    op("g", lambda: ENG["g"].dma_start(win_sb[:, :], win_h), writes=["win_sb"], inc=16)
    op("g", lambda: ENG["g"].dma_start(w2s_sb[:, :], w2s_h), writes=["w2s_sb"], inc=16)
    op("g", lambda: ENG["g"].dma_start(wout_sb[:, :], wout_h), writes=["wout_sb"], inc=16)
    op("y", lambda: ENG["y"].dma_start(bqc[:, :], bqc_h), writes=["bqc"], inc=16)
    op("y", lambda: ENG["y"].dma_start(bkc[:, :], bkc_h), writes=["bkc"], inc=16)
    op("y", lambda: ENG["y"].dma_start(binp[:, :], binp_h), writes=["binp"], inc=16)
    op("y", lambda: ENG["y"].dma_start(w1p[:, :], w1p_h), writes=["w1p"], inc=16)
    op("y", lambda: ENG["y"].dma_start(b1p[:, :], b1p_h), writes=["b1p"], inc=16)
    op("y", lambda: ENG["y"].dma_start(b2v[:, :], b2v_h), writes=["b2v"], inc=16)

    esX = ExitStack()
    xnt = esX.enter_context(nc.sbuf_tensor("xnt", [128, 16384], BF16))
    esA = ExitStack()
    sbufA = lambda n_, s_, d_: esA.enter_context(nc.sbuf_tensor(n_, s_, d_))
    win_sb = sbufA("win_sb", [128, 1024], BF16)
    w2s_sb = sbufA("w2s_sb", [128, 512], BF16)
    xin = sbufA("xin", [128, 4096], F32)
    xnrow = sbufA("xnrow", [128, 2048], BF16)
    s2 = sbufA("s2", [128, 2048], BF16)
    hb = sbufA("hb", [128, 2048], BF16)
    bvr = sbufA("bvr_s", [128, 1024], BF16)
    wv_sb = sbufA("wv_sb", [128, 8192], BF16)


    op("g", lambda: ENG["g"].dma_start(bvr[:, :], bvr_h), writes=["bvr"], inc=16)
    for i in range(4):
        op("g", lambda i=i: ENG["g"].dma_start(wv_sb[:, i * 2048:(i + 1) * 2048],
                                               wv_h[:, i * 2048:(i + 1) * 2048]),
           writes=[f"wv{i}"], inc=16)

    def xslab_ap(base):
        return x_h.rearrange("(a p) e -> p a e", p=128)[:, base:base + 2, :] if False else \
            bass.AP(x_h.tensor, base * 128 * 1024,
                    [[1024, 128], [128 * 1024, 2], [1, 1024]])

    def p1_a(t):
        c0 = (t % 2) * 8
        slab, sl2 = t // 2, (t // 2) % 2
        if t % 2 == 0:
            op("y", lambda slab=slab, sl2=sl2: ENG["y"].dma_start(
                xin[:, :].rearrange("p (a e) -> p a e", a=4)[:, sl2 * 2:sl2 * 2 + 2, :],
                xslab_ap(slab * 2)),
               writes=[f"xsl{sl2}"], inc=16)
        xoff = (sl2 * 2 + (t % 2)) * 1024
        op("s", lambda xoff=xoff, c0=c0: ENG["s"].activation(
            hb[:, 0:1024], xin[:, xoff:xoff + 1024], AF.Square,
            accum_out=lns[:, c0 + 1:c0 + 2]),
           reads=[f"xsl{sl2}"], writes=["hb0", f"lns_s2_{t % 2}"])
        op("v", lambda xoff=xoff, c0=c0: ENG["v"].reduce_sum(
            lns[:, c0 + 0:c0 + 1], xin[:, xoff:xoff + 1024], axis=AX.X),
           reads=[f"xsl{sl2}"], writes=[f"lns_s1_{t % 2}"])

    def p1_b(t):
        sl, c0 = t % 2, (t % 2) * 8
        op("v", lambda c0=c0: ENG["v"].tensor_scalar(
            lns[:, c0 + 2:c0 + 4], lns[:, c0 + 0:c0 + 2], 1.0 / 1024.0, None, op0=ALU.mult),
           reads=[f"lns_s1_{sl}", f"lns_s2_{sl}"], writes=[f"lns_m_{sl}", f"lns_e_{sl}"])
        op("v", lambda c0=c0: ENG["v"].tensor_tensor(
            lns[:, c0 + 4:c0 + 5], lns[:, c0 + 2:c0 + 3], lns[:, c0 + 2:c0 + 3], op=ALU.mult),
           reads=[f"lns_m_{sl}"], writes=[f"lns_m2_{sl}"])
        op("v", lambda c0=c0: ENG["v"].tensor_scalar(
            lns[:, c0 + 5:c0 + 6], lns[:, c0 + 3:c0 + 4], None, EPS,
            op0=ALU.subtract, op1=ALU.add) if False else ENG["v"].tensor_tensor(
            lns[:, c0 + 5:c0 + 6], lns[:, c0 + 3:c0 + 4], lns[:, c0 + 4:c0 + 5], op=ALU.subtract),
           reads=[f"lns_e_{sl}", f"lns_m2_{sl}"], writes=[f"lns_v_{sl}"])
        op("v", lambda c0=c0: ENG["v"].tensor_scalar(
            lns[:, c0 + 5:c0 + 6], lns[:, c0 + 5:c0 + 6], EPS, None, op0=ALU.add),
           reads=[f"lns_v_{sl}"], writes=[f"lns_v_{sl}"])
        op("v", lambda c0=c0: ENG["v"].reciprocal(
            lns[:, c0 + 6:c0 + 7], lns[:, c0 + 5:c0 + 6]),
           reads=[f"lns_v_{sl}"], writes=[f"lns_rv_{sl}"])
        op("s", lambda c0=c0: ENG["s"].activation(
            lns[:, c0 + 6:c0 + 7], lns[:, c0 + 6:c0 + 7], AF.Sqrt),
           reads=[f"lns_rv_{sl}"], writes=[f"lns_r_{sl}"])
        op("v", lambda c0=c0: ENG["v"].tensor_tensor(
            lns[:, c0 + 7:c0 + 8], lns[:, c0 + 2:c0 + 3], lns[:, c0 + 6:c0 + 7], op=ALU.mult),
           reads=[f"lns_m_{sl}", f"lns_r_{sl}"], writes=[f"lns_mr_{sl}"])
        op("v", lambda c0=c0: ENG["v"].tensor_scalar(
            lns[:, c0 + 7:c0 + 8], lns[:, c0 + 7:c0 + 8], -1.0, None, op0=ALU.mult),
           reads=[f"lns_mr_{sl}"], writes=[f"lns_nmr_{sl}"])

    def p1_c(t):
        sl, c0 = t % 2, (t % 2) * 8
        sl2 = (t // 2) % 2
        xoff = (sl2 * 2 + (t % 2)) * 1024
        op("s", lambda xoff=xoff, sl=sl, c0=c0: ENG["s"].activation(
            xnrow[:, sl * 1024:(sl + 1) * 1024], xin[:, xoff:xoff + 1024],
            AF.Identity, bias=lns[:, c0 + 7:c0 + 8], scale=lns[:, c0 + 6:c0 + 7]),
           reads=[f"xsl{sl2}", f"lns_r_{sl}", f"lns_nmr_{sl}"], writes=[f"xnrow{sl}"])
        bank = 6 + sl
        for fi in range(FI):
            op("t", lambda fi=fi, sl=sl, bank=bank: ENG["t"].transpose(
                pbf[bank][:, fi * 128:(fi + 1) * 128],
                xnrow[:, sl * 1024 + fi * 128: sl * 1024 + (fi + 1) * 128], ident[:, :]),
               reads=[f"xnrow{sl}", "ident"], writes=[f"pb{bank}"])
        def evac_xnt(t=t, bank=bank):
            dst = xnt[:, :].rearrange("p (c n) -> p c n", c=FI)[:, :, t * 128:(t + 1) * 128]
            srcp = pbf[bank][:, 0:1024].rearrange("p (c n) -> p c n", c=FI)
            return ENG["v"].tensor_copy(dst, srcp)
        op("v", evac_xnt, reads=[f"pb{bank}"], writes=[f"xnt{t}"])

    def vproj(j):
        tok0 = 128 * j - 64
        pstart = 64 if j == 0 else 0
        pcnt = 64 if j in (0, 16) else 128
        a0 = max(tok0, 0)
        tiles = sorted({a0 // 128, (a0 + pcnt - 1) // 128})
        xr = [f"xnt{tt}" for tt in tiles]
        bank0 = (j % 2) * 2
        for half in range(2):
            bank = bank0 + half
            for fi in range(FI):
                op("t", lambda j=j, half=half, fi=fi, bank=bank, a0=a0, pcnt=pcnt:
                   ENG["t"].matmul(
                       pb[bank][0:pcnt, :],
                       xnt[:, fi * 2048 + a0: fi * 2048 + a0 + pcnt],
                       wv_sb[:, fi * 1024 + half * 512: fi * 1024 + (half + 1) * 512],
                       start=(fi == 0), stop=(fi == FI - 1)),
                   reads=xr + [f"wv{fi // 2}"], writes=[f"pb{bank}"])
            jc = 0 if j == 16 else j
            op("v", lambda jc=jc, half=half, bank=bank, pstart=pstart, pcnt=pcnt, j=j:
               ENG["v"].tensor_tensor(
                   vtsh[pstart:pstart + pcnt, jc * 1024 + half * 512: jc * 1024 + (half + 1) * 512],
                   pb[bank][0:pcnt, :],
                   bvr[pstart:pstart + pcnt, half * 512:(half + 1) * 512], op=ALU.add),
               reads=[f"pb{bank}", "bvr"], writes=[f"vtsh{j}h{half}"])

    for t in range(NT + 6):
        if t < NT:
            p1_a(t)
        if 0 <= t - 1 < NT:
            p1_b(t - 1)
        if 0 <= t - 2 < NT:
            p1_c(t - 2)
        if PH >= 2 and 0 <= t - 3 <= 16:
            vproj(t - 3)

    if PH >= 2:
        for pi, (wh, dst, bias, nm) in enumerate(
                [(wq_h, qf, bqc, "q")]):
            for fo in range(FI):
                slot = (pi * 8 + fo) % 8
                op("g", lambda wh=wh, fo=fo, slot=slot: ENG["g"].dma_start(
                    wbig[:, slot * 1024:(slot + 1) * 1024], wh[:, fo * 1024:(fo + 1) * 1024]),
                   writes=[f"wslot{slot}"], inc=16)
                banks = [4 + (fo % 2) * 2, 5 + (fo % 2) * 2]
                for tc in range(4):
                    bank = banks[tc % 2]
                    bname = f"pb{bank}"
                    for fi in range(FI):
                        op("t", lambda fo=fo, tc=tc, fi=fi, slot=slot, bank=bank:
                           ENG["t"].matmul(
                               pb[bank][:, :],
                               wbig[:, slot * 1024 + fi * 128: slot * 1024 + (fi + 1) * 128],
                               xnt[:, fi * 2048 + tc * 512: fi * 2048 + (tc + 1) * 512],
                               start=(fi == 0), stop=(fi == FI - 1)),
                           reads=[f"wslot{slot}"] + [f"xnt{tt}" for tt in range(tc * 4, tc * 4 + 4)],
                           writes=[bname])
                    op("s", lambda dst=dst, bias=bias, fo=fo, tc=tc, bank=bank:
                       ENG["s"].activation(
                           dst[:, fo * 2048 + tc * 512: fo * 2048 + (tc + 1) * 512],
                           pb[bank][:, :], AF.Identity, bias=bias[:, fo:fo + 1], scale=1.0),
                       reads=[f"pb{bank}", "bqc" if nm == "q" else "bkc"],
                       writes=[f"{nm}f{fo}_{tc}"])

        for fo in range(FI):
            op("g", lambda fo=fo: ENG["g"].dma_start(
                wbig[:, fo * 1024:(fo + 1) * 1024], wk_h[:, fo * 1024:(fo + 1) * 1024]),
               writes=[f"wslot{fo}"], inc=16)

        for tc in range(4):
            bank = 0 + (tc % 2)
            for fi in range(FI):
                op("t", lambda tc=tc, fi=fi, bank=bank: ENG["t"].matmul(
                    pb[bank][:, :],
                    win_sb[:, fi * 128:(fi + 1) * 128],
                    xnt[:, fi * 2048 + tc * 512: fi * 2048 + (tc + 1) * 512],
                    start=(fi == 0), stop=(fi == FI - 1)),
                   reads=["win_sb"] + [f"xnt{tt}" for tt in range(tc * 4, tc * 4 + 4)],
                   writes=[f"pb{bank}"])
            op("s", lambda tc=tc, bank=bank: ENG["s"].activation(
                s2[:, tc * 512:(tc + 1) * 512], pb[bank][:, :], AF.Identity,
                bias=binp[:, :], scale=1.0),
               reads=[f"pb{bank}", "binp"], writes=[f"s2_{tc // 2}"])
        for jj in range(8):
            for hf in range(2):
                sl = (jj * 2 + hf) % 2
                op("s", lambda jj=jj, hf=hf, sl=sl: ENG["s"].activation(
                    hb[:, sl * 1024:(sl + 1) * 1024], s2[:, hf * 1024:(hf + 1) * 1024], AF.Relu,
                    bias=b1p[:, jj:jj + 1], scale=w1p[:, jj:jj + 1]),
                   reads=[f"s2_{hf}", "w1p", "b1p"], writes=[f"hb{sl}"])
                for q2 in range(2):
                    bank = 2 + (hf * 2 + q2) % 4
                    op("t", lambda jj=jj, hf=hf, sl=sl, q2=q2, bank=bank: ENG["t"].matmul(
                        pb[bank][0:64, :],
                        w2s_sb[:, jj * 64:(jj + 1) * 64],
                        hb[:, sl * 1024 + q2 * 512: sl * 1024 + (q2 + 1) * 512],
                        start=(jj == 0), stop=(jj == 7)),
                       reads=[f"hb{sl}", "w2s_sb"], writes=[f"pb{bank}"])
        for tc in range(4):
            bank = 2 + tc
            op("s", lambda tc=tc, bank=bank: ENG["s"].activation(
                g_sb[:, tc * 512:(tc + 1) * 512], pb[bank][0:64, :], AF.Identity,
                bias=b2v[:, :], scale=1.0),
               reads=[f"pb{bank}", "b2v"], writes=["g"])

    P.barrier()
    esA.close()

    esBkeep = ExitStack()
    afm = esBkeep.enter_context(nc.sbuf_tensor("afm", [128, 16384], BF16))
    esB = ExitStack()
    sbufB = lambda n_, s_, d_: esB.enter_context(nc.sbuf_tensor(n_, s_, d_))
    mms = sbufB("mms", [128, 256], BF16)
    m0s = mms[:, 64:256]
    m15s = mms[:, 0:192]
    svdgsv = sbufB("svdgsv", [33, 1024], BF16)
    svdg = svdgsv[0:4, :]
    svrow = svdgsv[32:33, :]
    emrt = sbufB("emrt", [4, 256], BF16)
    stt = sbufB("stt", [128, 112], F32)
    emrec = sbufB("emrec", [128, 16], BF16)
    wbuf = sbufB("wbuf", [128, 2048], F32)
    ebuf = sbufB("ebuf", [128, 2048], BF16)
    ppb = sbufB("ppb", [128, 2048], BF16)
    ppt = sbufB("ppt", [128, 2048], BF16)

    op("g", lambda: ENG["g"].dma_start(mms[:, :], maskm_h), writes=["mm", "m0", "m15"], inc=16)


    if PH >= 3:
        for half in range(2):
            for j in range(16):
                op("t", lambda j=j, half=half: ENG["t"].matmul(
                    pb[6 + half][0:1, 0:512],
                    ones_b[:, :],
                    vtsh[:, j * 1024 + half * 512: j * 1024 + (half + 1) * 512],
                    start=(j == 0), stop=(j == 15)),
                   reads=[f"vtsh{j}h{half}", "ones_b"] +
                         ([f"vtsh16h{half}"] if j == 0 else []),
                   writes=[f"pb{6 + half}"])
        for half in range(2):
            op("v", lambda half=half: ENG["v"].tensor_copy(
                svrow[:, half * 512:(half + 1) * 512], pb[6 + half][0:1, :]),
               reads=[f"pb{6 + half}"], writes=["svrow"])
        op("v", lambda: ENG["v"].memset(svdg[:, :], 0.0), writes=["svdg"])
        for h in range(H):
            g_, r_ = h // 4, h % 4
            op("g", lambda h=h, g_=g_, r_=r_: ENG["g"].dma_start(
                svdg[r_:r_ + 1, g_ * 256 + r_ * 64: g_ * 256 + (r_ + 1) * 64],
                svrow[0:1, h * 64:(h + 1) * 64]),
               reads=["svrow"], writes=["svdg"], inc=16)

        def qt_params(qt):
            if qt == 0:
                return 192, "m0", m0s, 0, [(0, 64, 0, 64), (64, 128, 1, 0)]
            if qt == NT - 1:
                return 192, "m15", m15s, 1856, [(0, 128, 15, 0), (128, 64, 16, 0)]
            return 256, "mm", mms, 128 * qt - 64, [(0, 128, qt, 0), (128, 128, qt + 1, 0)]

        NTASK = NT * 4

        def stage_scores(k):
            qt, g_ = divmod(k, 4)
            W, mname, mask, wstart, pieces = qt_params(qt)
            pair = (0, 1) if k % 2 == 0 else (2, 3)
            for j in range(4):
                h = 4 * g_ + j
                hc, hbase = h // 2, (h % 2) * 64
                bank = pair[j % 2]
                col = (j // 2) * 256
                op("t", lambda h=h, hc=hc, hbase=hbase, qt=qt, W=W, wstart=wstart,
                       bank=bank, col=col:
                   ENG["t"].matmul(
                       pb[bank][:, col:col + W],
                       qf[hbase:hbase + 64, hc * 2048 + qt * 128: hc * 2048 + (qt + 1) * 128],
                       kf[hbase:hbase + 64, hc * 2048 + wstart: hc * 2048 + wstart + W],
                       start=True, stop=True, tile_position=(hbase, 0)),
                   reads=[f"qf{hc}_{qt // 4}"] +
                         [f"kf{hc}_{tc}" for tc in range(wstart // 512, (wstart + W - 1) // 512 + 1)],
                   writes=[f"pb{bank}"], trows=(hbase, hbase + 64))

        def stage_sm_a(k):
            qt, g_ = divmod(k, 4)
            W, mname, mask, wstart, pieces = qt_params(qt)
            pair = (0, 1) if k % 2 == 0 else (2, 3)
            c0 = 4 * g_
            for j in range(4):
                h = c0 + j
                bank, col = pair[j % 2], (j // 2) * 256
                op("v", lambda h=h, bank=bank, col=col, W=W, mask=mask:
                   ENG["v"].tensor_tensor_reduce(
                       wbuf[:, (h % 8) * 256: (h % 8) * 256 + W],
                       pb[bank][:, col:col + W],
                       mask[:, 0:W], 1.0, 0.0, ALU.mult, ALU.max,
                       accum_out=stt[:, h:h + 1]),
                   reads=[f"pb{bank}", mname], writes=[f"wS{h % 8}", f"mb{h}"])
            op("v", lambda c0=c0: ENG["v"].tensor_scalar(
                stt[:, 16 + c0:16 + c0 + 4], stt[:, c0:c0 + 4], -1.0, None, op0=ALU.mult),
               reads=[f"mb{h}" for h in range(c0, c0 + 4)], writes=[f"negmG{g_}"])

        def stage_sm_b(k):
            qt, g_ = divmod(k, 4)
            W, mname, mask, wstart, pieces = qt_params(qt)
            c0 = 4 * g_
            op("s", lambda c0=c0: ENG["s"].activation(
                stt[:, 48 + c0:48 + c0 + 4], stt[:, 16 + c0:16 + c0 + 4], AF.Exp),
               reads=[f"negmG{g_}"], writes=[f"emG{g_}"])
            for j in range(4):
                h = c0 + j
                op("s", lambda h=h, W=W: ENG["s"].activation(
                    ebuf[:, (h % 8) * 256: (h % 8) * 256 + W], wbuf[:, (h % 8) * 256: (h % 8) * 256 + W], AF.Exp,
                    bias=stt[:, 16 + h:16 + h + 1], scale=1.0,
                    accum_out=stt[:, 32 + h:32 + h + 1]),
                   reads=[f"wS{h % 8}", f"negmG{g_}"], writes=[f"eS{h % 8}", f"sp{h}"])

        def stage_sm_c(k):
            qt, g_ = divmod(k, 4)
            W, mname, mask, wstart, pieces = qt_params(qt)
            c0 = 4 * g_
            op("v", lambda c0=c0, W=W: ENG["v"].tensor_scalar(
                stt[:, 64 + c0:64 + c0 + 4], stt[:, 48 + c0:48 + c0 + 4],
                float(N - W), None, op0=ALU.mult),
               reads=[f"emG{g_}"], writes=[f"ztG{g_}"])
            op("v", lambda c0=c0: ENG["v"].tensor_tensor(
                stt[:, 80 + c0:80 + c0 + 4], stt[:, 64 + c0:64 + c0 + 4],
                stt[:, 32 + c0:32 + c0 + 4], op=ALU.add),
               reads=[f"ztG{g_}"] + [f"sp{h}" for h in range(c0, c0 + 4)],
               writes=[f"zG{g_}"])
            op("v", lambda c0=c0: ENG["v"].reciprocal(
                stt[:, 96 + c0:96 + c0 + 4], stt[:, 80 + c0:80 + c0 + 4]),
               reads=[f"zG{g_}"], writes=[f"rzG{g_}"])
            op("v", lambda c0=c0: ENG["v"].tensor_tensor(
                emrec[:, c0:c0 + 4], stt[:, 48 + c0:48 + c0 + 4],
                stt[:, 96 + c0:96 + c0 + 4], op=ALU.mult),
               reads=[f"emG{g_}", f"rzG{g_}"], writes=[f"emrG{g_}"])
            for j in range(4):
                h = c0 + j
                op("v", lambda h=h, W=W: ENG["v"].tensor_scalar(
                    ppb[:, (h % 8) * 256: (h % 8) * 256 + W], ebuf[:, (h % 8) * 256: (h % 8) * 256 + W],
                    stt[:, 48 + h:48 + h + 1], stt[:, 96 + h:96 + h + 1],
                    op0=ALU.subtract, op1=ALU.mult),
                   reads=[f"eS{h % 8}", f"emG{g_}", f"rzG{g_}"], writes=[f"ppS{h % 8}"])

        def stage_pt(k):
            qt, g_ = divmod(k, 4)
            W, mname, mask, wstart, pieces = qt_params(qt)
            bank = 4
            ev = "v" if qt % 2 == 0 else "s"
            c0 = 4 * g_
            op("t", lambda c0=c0, bank=bank: ENG["t"].transpose(
                pbf[bank][0:4, 0:128], emrec[:, c0:c0 + 4], ident[:, :]),
               reads=[f"emrG{g_}", "ident"], writes=[f"pb{bank}"])
            op(ev, lambda bank=bank, ev=ev, k=k: (ENG["v"].tensor_copy if ev == "v" else ENG["s"].copy)(
                emrt[0:4, (k % 2) * 128:(k % 2) * 128 + 128], pbf[bank][0:4, 0:128]),
               reads=[f"pb{bank}"], writes=[f"emrt{k % 2}"])
            if qt in (0, NT - 1):
                for j in range(4):
                    if qt == 0:
                        zap = pb[bank][0:64, j * 128: j * 128 + 64]
                    else:
                        zap = pb[bank][64:128, j * 128 + 64: j * 128 + 128]
                    op("v", lambda zap=zap: ENG["v"].memset(zap, 0.0),
                       reads=[], writes=[f"pb{bank}"])
            for j in range(4):
                h = c0 + j
                for i, (cc0, cw, vj, pstart) in enumerate(pieces):
                    op("t", lambda h=h, j=j, i=i, cc0=cc0, cw=cw, pstart=pstart, bank=bank:
                       ENG["t"].transpose(
                           pbf[bank][pstart:pstart + cw, j * 256 + i * 128: j * 256 + (i + 1) * 128],
                           ppb[:, (h % 8) * 256 + cc0: (h % 8) * 256 + cc0 + cw],
                           ident[:, :], tile_position=(0, pstart)),
                       reads=[f"ppS{h % 8}", "ident"], writes=[f"pb{bank}"])
            op(ev, lambda bank=bank, k=k, ev=ev: (ENG["v"].tensor_copy if ev == "v" else ENG["s"].copy)(
                ppt[:, (k % 2) * 1024:(k % 2) * 1024 + 1024], pbf[bank][:, 0:1024]),
               reads=[f"pb{bank}"], writes=[f"ppts{k % 2}"])

        def stage_pv(k):
            qt, g_ = divmod(k, 4)
            W, mname, mask, wstart, pieces = qt_params(qt)
            c0 = 4 * g_
            ev = "v" if qt % 2 == 0 else "s"

            def pv_corr(j):
                h, bankp = c0 + j, 6 + (j % 2)
                op("t", lambda h=h, g_=g_, j=j, bankp=bankp, k=k: ENG["t"].matmul(
                    pb[bankp][0:64, 0:128],
                    svdg[0:4, g_ * 256 + j * 64: g_ * 256 + (j + 1) * 64],
                    emrt[0:4, (k % 2) * 128:(k % 2) * 128 + 128],
                    start=True, stop=False, tile_position=(0, 0)),
                   reads=["svdg", f"emrt{k % 2}"], writes=[f"pb{bankp}"], trows=(0, 4))

            def pv_piece(j, i):
                h, bankp = c0 + j, 6 + (j % 2)
                cc0, cw, vj, pstart = pieces[i]
                op("t", lambda h=h, j=j, i=i, cw=cw, vj=vj, pstart=pstart,
                       bankp=bankp, k=k:
                   ENG["t"].matmul(
                       pb[bankp][0:64, 0:128],
                       vtsh[pstart:pstart + cw,
                            (0 if vj == 16 else vj) * 1024 + h * 64:
                            (0 if vj == 16 else vj) * 1024 + (h + 1) * 64],
                       ppt[pstart:pstart + cw,
                           (k % 2) * 1024 + j * 256 + i * 128: (k % 2) * 1024 + j * 256 + (i + 1) * 128],
                       start=False, stop=(i == 1), tile_position=(pstart, 0)),
                   reads=[f"ppts{k % 2}", f"vtsh{vj}h{h // 8}"], writes=[f"pb{bankp}"],
                   trows=(pstart, pstart + cw))

            def pv_evac(j):
                h, bankp = c0 + j, 6 + (j % 2)
                cch = 2 * g_ + j // 2
                op(ev, lambda qt=qt, h=h, cch=cch, bankp=bankp, ev=ev:
                   (ENG["v"].tensor_copy if ev == "v" else ENG["s"].copy)(
                       afm[(h % 2) * 64:(h % 2) * 64 + 64,
                           cch * 2048 + qt * 128: cch * 2048 + (qt + 1) * 128],
                       pb[bankp][0:64, 0:128]),
                   reads=[f"pb{bankp}"], writes=[f"afm{qt}_{g_ // 2}"])

            for jp in (0, 2):
                pv_corr(jp); pv_corr(jp + 1)
                pv_piece(jp, 0); pv_piece(jp + 1, 0)
                pv_piece(jp, 1); pv_piece(jp + 1, 1)
                pv_evac(jp); pv_evac(jp + 1)

        def kproj_group(fo, tc):
            for fi in range(FI):
                op("t", lambda fo=fo, tc=tc, fi=fi: ENG["t"].matmul(
                    pb[5][:, :],
                    wbig[:, fo * 1024 + fi * 128: fo * 1024 + (fi + 1) * 128],
                    xnt[:, fi * 2048 + tc * 512: fi * 2048 + (tc + 1) * 512],
                    start=(fi == 0), stop=(fi == FI - 1)),
                   reads=[f"wslot{fo}"] + [f"xnt{tt}" for tt in range(tc * 4, tc * 4 + 4)],
                   writes=["pb5"])
            op("s", lambda fo=fo, tc=tc: ENG["s"].activation(
                kf[:, fo * 2048 + tc * 512: fo * 2048 + (tc + 1) * 512],
                pb[5][:, :], AF.Identity, bias=bkc[:, fo:fo + 1], scale=1.0),
               reads=["pb5", "bkc"], writes=[f"kf{fo}_{tc}"])

        for fo in range(FI):
            kproj_group(fo, 0)

        for k in range(NTASK + 5):
            if k < NTASK:
                stage_scores(k)
            if 0 <= k - 1 < NTASK:
                stage_sm_a(k - 1)
            if 0 <= k - 2 < NTASK:
                stage_sm_b(k - 2)
            if 0 <= k - 3 < NTASK:
                stage_sm_c(k - 3)
            if 0 <= k - 4 < NTASK:
                stage_pt(k - 4)
            if 0 <= k - 5 < NTASK:
                stage_pv(k - 5)
            if k % 2 == 0 and k < 48:
                kproj_group((k % 16) // 2, 1 + k // 16)
            if k >= 48 and k < 56 and k % 2 == 0:
                i = (k - 48) // 2
                op("g", lambda i=i: ENG["g"].dma_start(
                    wbig[:, i * 2048:(i + 1) * 2048], wo_h[:, i * 2048:(i + 1) * 2048]),
                   writes=[f"wslot{2 * i}", f"wslot{2 * i + 1}"], inc=16)

    P.barrier()
    esB.close()
    qfF = qf[:, :].bitcast(F32)
    kfF = kf[:, :].bitcast(F32)
    xin4 = qfF[:, 0:4096]
    xb = qfF[:, 4096:7168]
    outs = kfF[:, 0:2048]
    bcomb = kfF[:, 2048:3072]

    op("y", lambda: ENG["y"].dma_start(outs[:, 0:1024], bor_h), writes=["outs0"], inc=16)
    op("y", lambda: ENG["y"].dma_start(outs[:, 1024:2048], boutr_h), writes=["outs1"], inc=16)
    op("v", lambda: ENG["v"].tensor_tensor(bcomb[:, :], outs[:, 0:1024], outs[:, 1024:2048],
                                           op=ALU.add),
       reads=["outs0", "outs1"], writes=["bcomb"])

    if PH >= 4:
        def p4_a(t):
            slab, sl2 = t // 2, (t // 2) % 2
            if t % 2 == 0:
                op("g", lambda slab=slab, sl2=sl2: ENG["g"].dma_start(
                    xin4[:, :].rearrange("p (a e) -> p a e", a=4)[:, sl2 * 2:sl2 * 2 + 2, :],
                    xslab_ap(slab * 2)),
                   writes=[f"xsl{sl2}"], inc=16)
            xoff = (sl2 * 2 + (t % 2)) * 1024
            op("v", lambda t=t, xoff=xoff, sl2=sl2: ENG["v"].tensor_tensor(
                xb[:, (t % 3) * 1024:(t % 3 + 1) * 1024],
                xin4[:, xoff:xoff + 1024], bcomb[:, :], op=ALU.add),
               reads=[f"xsl{sl2}", "bcomb"], writes=[f"xb{t % 3}"])

        def p4_b(t):
            for hf in range(2):
                bank = (t % 2) * 2 + hf
                for c in range(FI):
                    op("t", lambda t=t, hf=hf, c=c, bank=bank: ENG["t"].matmul(
                        pb[bank][:, :],
                        afm[:, c * 2048 + t * 128: c * 2048 + (t + 1) * 128],
                        wbig[:, c * 1024 + hf * 512: c * 1024 + (hf + 1) * 512],
                        start=(c == 0), stop=False),
                       reads=[f"afm{t}_{c // 4}", f"wslot{c}"], writes=[f"pb{bank}"])
                op("t", lambda t=t, hf=hf, bank=bank: ENG["t"].matmul(
                    pb[bank][:, :],
                    g_sb[:, t * 128:(t + 1) * 128],
                    wout_sb[:, hf * 512:(hf + 1) * 512],
                    start=False, stop=True),
                   reads=["g", "wout_sb"], writes=[f"pb{bank}"], trows=(0, 64))

        def p4_c(t):
            sl = t % 2
            for hf in range(2):
                bank = (t % 2) * 2 + hf
                op("v", lambda t=t, sl=sl, hf=hf, bank=bank: ENG["v"].tensor_tensor(
                    outs[:, sl * 1024 + hf * 512: sl * 1024 + (hf + 1) * 512],
                    pb[bank][:, :], xb[:, (t % 3) * 1024 + hf * 512: (t % 3) * 1024 + (hf + 1) * 512],
                    op=ALU.add),
                   reads=[f"pb{bank}", f"xb{t % 3}"], writes=[f"outs{sl}"])
            op("y", lambda t=t, sl=sl: ENG["y"].dma_start(
                y_h[t * 128:(t + 1) * 128, :], outs[:, sl * 1024:(sl + 1) * 1024]),
               reads=[f"outs{sl}"], writes=[], inc=16)

        for t in range(NT + 2):
            if 0 <= t - 2 < NT:
                p4_c(t - 2)
            if t < NT:
                p4_a(t)
            if 0 <= t - 1 < NT:
                p4_b(t - 1)

    P.final_waits()

    semh = {}
    for s_ in P.sems:
        semh[s_] = es.enter_context(nc.semaphore(f"sem_{s_}"))
    block = es.enter_context(nc.Block())

    def replay(e, engobj):
        ENG[e] = engobj
        for item in P.stream[e]:
            if item[0] == "wait":
                engobj.wait_ge(semh[item[1]], item[2])
            else:
                instr = item[1]()
                instr.then_inc(semh[item[3]], item[2])

    @block.tensor
    def _(t):
        replay("t", t)

    @block.vector
    def _(v):
        replay("v", v)

    @block.scalar
    def _(s):
        replay("s", s)

    @block.sync
    def _(s):
        replay("y", s)

    @block.gpsimd
    def _(g):
        replay("g", g)

    esBkeep.close()
    esX.close()
    es.close()
    lower_extended_insts(nc)
    return nc


def _host_consts(i):
    f = np.ascontiguousarray
    Wq = np.asarray(i["L0_attn_Wq"], np.float32)
    Wk = np.asarray(i["L0_attn_Wk"], np.float32)
    Wv = np.asarray(i["L0_attn_Wv"], np.float32)
    Wo = np.asarray(i["L0_attn_Wo"], np.float32)
    bq = np.asarray(i["L0_attn_bq"], np.float32)
    bk = np.asarray(i["L0_attn_bk"], np.float32)
    bv = np.asarray(i["L0_attn_bv"], np.float32)
    bo = np.asarray(i["L0_attn_bo"], np.float32)
    Win = np.asarray(i["L0_ffn_Win"], np.float32)
    bin_ = np.asarray(i["L0_ffn_bin_"], np.float32)
    W1 = np.asarray(i["L0_ffn_W1"], np.float32)
    b1 = np.asarray(i["L0_ffn_b1"], np.float32)
    W2 = np.asarray(i["L0_ffn_W2"], np.float32)
    b2 = np.asarray(i["L0_ffn_b2"], np.float32)
    Wout = np.asarray(i["L0_ffn_Wout"], np.float32)
    bout = np.asarray(i["L0_ffn_bout"], np.float32)

    d = {}
    d["wq"] = f(Wq.reshape(8, 128, 8, 128).transpose(1, 2, 0, 3).reshape(128, 8192))
    d["wk"] = f(Wk.reshape(8, 128, 8, 128).transpose(1, 2, 0, 3).reshape(128, 8192))
    d["wv"] = f(Wv.reshape(8, 128, 1024).transpose(1, 0, 2).reshape(128, 8192))
    d["wo"] = f(Wo.reshape(8, 128, 1024).transpose(1, 0, 2).reshape(128, 8192))
    d["bqc"] = f(bq.reshape(8, 128).T)
    d["bkc"] = f(bk.reshape(8, 128).T)
    d["bvr"] = f(np.broadcast_to(bv[None, :], (128, 1024)))
    wind = Win.reshape(8, 128, 64)
    d["win"] = f(np.concatenate([wind, wind], axis=2).transpose(1, 0, 2).reshape(128, 1024))
    d["binp"] = f(np.concatenate([bin_, bin_]).reshape(128, 1))
    d["w1p"] = f(np.concatenate([W1[:, 0::2], W1[:, 1::2]], axis=0))
    d["b1p"] = f(np.concatenate([b1[:, 0::2], b1[:, 1::2]], axis=0))
    w2s = np.zeros((128, 512), np.float32)
    for m in range(64):
        for j in range(8):
            w2s[m, j * 64 + m] = W2[m, 2 * j]
            w2s[64 + m, j * 64 + m] = W2[m, 2 * j + 1]
    d["w2s"] = w2s
    d["b2v"] = f(b2.reshape(64, 1))
    d["wout"] = f(Wout)
    d["bor"] = f(np.broadcast_to(bo[None, :], (128, 1024)))
    d["boutr"] = f(np.broadcast_to(bout[None, :], (128, 1024)))
    r = np.arange(128)[:, None]
    c0 = np.arange(192)[None, :]
    cm = np.arange(256)[None, :]
    d["mask0"] = (np.abs(c0 - r) <= 64).astype(np.float32)
    d["maskm"] = ((cm >= r) & (cm <= r + 128)).astype(np.float32)
    d["mask15"] = ((c0 >= r) & (c0 <= r + 128)).astype(np.float32)
    d["identc"] = np.eye(128, dtype=np.float32)
    d["onesc"] = np.ones((128, 1), np.float32)
    return d


_NC_CACHE = None


def kernel(**inputs):
    global _NC_CACHE
    x = np.ascontiguousarray(np.asarray(inputs["L0_attn_x"], np.float32))
    consts = _host_consts(inputs)
    if _NC_CACHE is None:
        _NC_CACHE = build_nc()
    nc = _NC_CACHE
    in_maps = []
    for c in range(NCORES):
        m = {"x": np.ascontiguousarray(x[c])}
        m.update(consts)
        in_maps.append(m)
    res = run_bass_kernel_spmd(nc, in_maps, core_ids=list(range(NCORES)))
    return np.stack([res.results[c]["y"] for c in range(NCORES)], axis=0)


if __name__ == "__main__":
    import reference as _r
    ins = _r.setup_inputs()
    ins = {k: np.asarray(v) for k, v in ins.items()}
    out = kernel(**ins)
    ref = np.asarray(_r.reference(**{k: v for k, v in ins.items()}))
    err = np.abs(out - ref)
    rv = float(((out - ref) ** 2).mean() / (ref ** 2).mean())
    print("max abs err:", err.max(), "resid var ratio:", rv)


# version 20
# speedup vs baseline: 1.0493x; 1.0293x over previous
import os
import numpy as np
from contextlib import ExitStack

import concourse.bass as bass
import concourse.mybir as mybir
from concourse.bass_utils import run_bass_kernel_spmd
from concourse.library_overlay import lower_extended_insts

F32 = mybir.dt.float32
BF16 = mybir.dt.bfloat16
AF = mybir.ActivationFunctionType
ALU = mybir.AluOpType
AX = mybir.AxisListType

N, E, H, D = 2048, 1024, 16, 64
NT = N // 128
FI = E // 128
NCORES = 8
EPS = 1e-5
M_FFN, INNER = 64, 16


class Prog:
    ENGS = ("t", "v", "s", "y", "g")
    QUEUES = ("y", "g")
    NLANES = 4

    def __init__(self):
        self.stream = {e: [] for e in self.ENGS}
        self.sems = ["t", "v", "s"] + [f"{q}{i}" for q in self.QUEUES for i in range(self.NLANES)]
        self.cnt = {s_: 0 for s_ in self.sems}
        self.lastwait = {e: {} for e in self.ENGS}
        self.qlane = {q: 0 for q in self.QUEUES}
        self.writer = {}
        self.readers = {}
        self.stream_of_sem = {}
        for e in ("t", "v", "s"):
            self.stream_of_sem[e] = e
        for q in self.QUEUES:
            for i in range(self.NLANES):
                self.stream_of_sem[f"{q}{i}"] = q

    def _need(self, waits, sem, val):
        if val > waits.get(sem, 0):
            waits[sem] = val

    def emit(self, eng, fn, reads=(), writes=(), inc=1, trows=(0, 128)):
        is_q = eng in self.QUEUES
        if is_q:
            lane = self.qlane[eng]
            self.qlane[eng] = (lane + 1) % self.NLANES
            mysem = f"{eng}{lane}"
        else:
            mysem = eng
        waits = {}
        if eng == "t":
            for b in writes:
                if b.startswith("pb"):
                    prev = getattr(self, "psum_rows", {}).get(b)
                    if prev is not None and (trows[1] <= prev[0] or prev[1] <= trows[0]):
                        self._need(waits, "t", prev[2])
            if not hasattr(self, "psum_rows"):
                self.psum_rows = {}
            for b in writes:
                if b.startswith("pb"):
                    self.psum_rows[b] = (trows[0], trows[1], self.cnt["t"] + 1)
        if is_q and self.cnt[mysem] > 0:
            self._need(waits, mysem, self.cnt[mysem])
        for b in reads:
            w = self.writer.get(b)
            if w is not None:
                ws, wc = w
                same_stream = self.stream_of_sem[ws] == eng
                if (not same_stream) or is_q or self.stream_of_sem[ws] in ("v", "s", "g"):
                    self._need(waits, ws, wc)
        for b in writes:
            for rs, rc in self.readers.get(b, {}).items():
                if self.stream_of_sem[rs] != eng or is_q or eng in ("v", "s", "g"):
                    self._need(waits, rs, rc)
            w = self.writer.get(b)
            if w is not None:
                ws, wc = w
                same_stream = self.stream_of_sem[ws] == eng
                if (not same_stream) or is_q or (eng in ("v", "s") and same_stream):
                    self._need(waits, ws, wc)
        for sem, val in waits.items():
            if val > self.lastwait[eng].get(sem, 0):
                self.stream[eng].append(("wait", sem, val))
                self.lastwait[eng][sem] = val
        self.stream[eng].append(("op", fn, inc, mysem))
        self.cnt[mysem] += inc
        c = self.cnt[mysem]
        for b in reads:
            self.readers.setdefault(b, {})[mysem] = c
        for b in writes:
            self.writer[b] = (mysem, c)
            self.readers[b] = {}

    def barrier(self):
        for e in self.ENGS:
            for sem in self.sems:
                own = (self.stream_of_sem[sem] == e) and e not in self.QUEUES
                if not own and self.cnt[sem] > self.lastwait[e].get(sem, 0):
                    self.stream[e].append(("wait", sem, self.cnt[sem]))
                    self.lastwait[e][sem] = self.cnt[sem]

    def final_waits(self):
        for q in self.QUEUES:
            for i in range(self.NLANES):
                sem = f"{q}{i}"
                if self.cnt[sem] > self.lastwait[q].get(sem, 0):
                    self.stream[q].append(("wait", sem, self.cnt[sem]))
                    self.lastwait[q][sem] = self.cnt[sem]


def build_nc():
    PH = int(os.environ.get('KPH', '4'))
    nc = bass.Bass("TRN2", target_bir_lowering=False)

    dt_in = lambda n_, s_: nc.dram_tensor(n_, s_, F32, kind="ExternalInput").ap()
    x_h = dt_in("x", [N, E])
    wq_h = dt_in("wq", [128, 8192])
    wk_h = dt_in("wk", [128, 8192])
    wv_h = dt_in("wv", [128, 8192])
    wo_h = dt_in("wo", [128, 8192])
    bqc_h = dt_in("bqc", [128, 8])
    bkc_h = dt_in("bkc", [128, 8])
    bvr_h = dt_in("bvr", [128, 1024])
    win_h = dt_in("win", [128, 1024])
    binp_h = dt_in("binp", [128, 1])
    w1p_h = dt_in("w1p", [128, 8])
    b1p_h = dt_in("b1p", [128, 8])
    w2s_h = dt_in("w2s", [128, 512])
    b2v_h = dt_in("b2v", [64, 1])
    wout_h = dt_in("wout", [64, 1024])
    bor_h = dt_in("bor", [128, 1024])
    boutr_h = dt_in("boutr", [128, 1024])
    mask0_h = dt_in("mask0", [128, 192])
    maskm_h = dt_in("maskm", [128, 256])
    mask15_h = dt_in("mask15", [128, 192])
    ident_h = dt_in("identc", [128, 128])
    ones_h = dt_in("onesc", [128, 1])
    y_h = nc.dram_tensor("y", [N, E], F32, kind="ExternalOutput").ap()

    es = ExitStack()
    sbuf = lambda n_, s_, d_: es.enter_context(nc.sbuf_tensor(n_, s_, d_))
    psum = lambda n_, s_, d_: es.enter_context(nc.psum_tensor(n_, s_, d_))

    qf = sbuf("qf", [128, 16384], BF16)
    kf = sbuf("kf", [128, 16384], BF16)
    vtsh = sbuf("vtsh", [128, 16 * 1024], BF16)
    wbig = sbuf("wbig", [128, 8192], BF16)
    ident = sbuf("ident", [128, 128], BF16)
    ones_b = sbuf("ones_b", [128, 1], BF16)
    wout_sb = sbuf("wout_sb", [64, 1024], BF16)
    bqc = sbuf("bqc_s", [128, 8], F32)
    bkc = sbuf("bkc_s", [128, 8], F32)
    binp = sbuf("binp_s", [128, 1], F32)
    w1p = sbuf("w1p_s", [128, 8], F32)
    b1p = sbuf("b1p_s", [128, 8], F32)
    b2v = sbuf("b2v_s", [64, 1], F32)
    g_sb = sbuf("g_sb", [64, 2048], BF16)
    lns = sbuf("lns", [128, 32], F32)

    pb = [psum(f"pb{i}", [128, 512], F32) for i in range(8)]
    pbf = [b.ap().bitcast(BF16) for b in pb]

    sems = {}
    P = Prog()

    ENG = {}

    def op(eng, fn, reads=(), writes=(), inc=1, trows=(0, 128)):
        P.emit(eng, fn, reads=reads, writes=writes, inc=inc, trows=trows)

    op("g", lambda: ENG["g"].dma_start(ident[:, :], ident_h), writes=["ident"], inc=16)
    op("g", lambda: ENG["g"].dma_start(ones_b[:, :], ones_h), writes=["ones_b"], inc=16)
    op("g", lambda: ENG["g"].dma_start(win_sb[:, :], win_h), writes=["win_sb"], inc=16)
    op("g", lambda: ENG["g"].dma_start(w2s_sb[:, :], w2s_h), writes=["w2s_sb"], inc=16)
    op("g", lambda: ENG["g"].dma_start(wout_sb[:, :], wout_h), writes=["wout_sb"], inc=16)
    op("y", lambda: ENG["y"].dma_start(bqc[:, :], bqc_h), writes=["bqc"], inc=16)
    op("y", lambda: ENG["y"].dma_start(bkc[:, :], bkc_h), writes=["bkc"], inc=16)
    op("y", lambda: ENG["y"].dma_start(binp[:, :], binp_h), writes=["binp"], inc=16)
    op("y", lambda: ENG["y"].dma_start(w1p[:, :], w1p_h), writes=["w1p"], inc=16)
    op("y", lambda: ENG["y"].dma_start(b1p[:, :], b1p_h), writes=["b1p"], inc=16)
    op("y", lambda: ENG["y"].dma_start(b2v[:, :], b2v_h), writes=["b2v"], inc=16)

    esX = ExitStack()
    xnt = esX.enter_context(nc.sbuf_tensor("xnt", [128, 16384], BF16))
    esA = ExitStack()
    sbufA = lambda n_, s_, d_: esA.enter_context(nc.sbuf_tensor(n_, s_, d_))
    win_sb = sbufA("win_sb", [128, 1024], BF16)
    w2s_sb = sbufA("w2s_sb", [128, 512], BF16)
    xin = sbufA("xin", [128, 4096], F32)
    xnrow = sbufA("xnrow", [128, 2048], BF16)
    s2 = sbufA("s2", [128, 2048], BF16)
    hb = sbufA("hb", [128, 2048], BF16)
    bvr = sbufA("bvr_s", [128, 1024], BF16)
    wv_sb = sbufA("wv_sb", [128, 8192], BF16)


    op("g", lambda: ENG["g"].dma_start(bvr[:, :], bvr_h), writes=["bvr"], inc=16)
    for i in range(4):
        op("g", lambda i=i: ENG["g"].dma_start(wv_sb[:, i * 2048:(i + 1) * 2048],
                                               wv_h[:, i * 2048:(i + 1) * 2048]),
           writes=[f"wv{i}"], inc=16)

    def xslab_ap(base):
        return x_h.rearrange("(a p) e -> p a e", p=128)[:, base:base + 2, :] if False else \
            bass.AP(x_h.tensor, base * 128 * 1024,
                    [[1024, 128], [128 * 1024, 2], [1, 1024]])

    def p1_a(t):
        c0 = (t % 2) * 16
        slab, sl2 = t // 2, (t // 2) % 2
        if t % 2 == 0:
            op("y", lambda slab=slab, sl2=sl2: ENG["y"].dma_start(
                xin[:, :].rearrange("p (a e) -> p a e", a=4)[:, sl2 * 2:sl2 * 2 + 2, :],
                xslab_ap(slab * 2)),
               writes=[f"xsl{sl2}"], inc=16)
        xoff = (sl2 * 2 + (t % 2)) * 1024
        for half in range(2):
            op("v", lambda xoff=xoff, c0=c0, half=half: ENG["v"].bn_stats(
                lns[:, c0 + 6 * half:c0 + 6 * half + 6],
                xin[:, xoff + half * 512:xoff + (half + 1) * 512]),
               reads=[f"xsl{sl2}"], writes=[f"lns_s{half}_{t % 2}"])

    def p1_b(t):
        sl, c0 = t % 2, (t % 2) * 16
        op("v", lambda c0=c0: ENG["v"].bn_aggr(
            lns[:, c0 + 12:c0 + 14], lns[:, c0 + 0:c0 + 12]),
           reads=[f"lns_s0_{sl}", f"lns_s1_{sl}"], writes=[f"lns_mv_{sl}"])
        op("v", lambda c0=c0: ENG["v"].tensor_scalar(
            lns[:, c0 + 13:c0 + 14], lns[:, c0 + 13:c0 + 14], EPS, None, op0=ALU.add),
           reads=[f"lns_mv_{sl}"], writes=[f"lns_ve_{sl}"])
        op("v", lambda c0=c0: ENG["v"].reciprocal(
            lns[:, c0 + 14:c0 + 15], lns[:, c0 + 13:c0 + 14]),
           reads=[f"lns_ve_{sl}"], writes=[f"lns_rv_{sl}"])
        op("s", lambda c0=c0: ENG["s"].activation(
            lns[:, c0 + 15:c0 + 16], lns[:, c0 + 14:c0 + 15], AF.Sqrt),
           reads=[f"lns_rv_{sl}"], writes=[f"lns_r_{sl}"])

    def p1_c(t):
        sl, c0 = t % 2, (t % 2) * 16
        sl2 = (t // 2) % 2
        xoff = (sl2 * 2 + (t % 2)) * 1024
        op("v", lambda xoff=xoff, sl=sl, c0=c0: ENG["v"].tensor_scalar(
            xnrow[:, sl * 1024:(sl + 1) * 1024], xin[:, xoff:xoff + 1024],
            lns[:, c0 + 12:c0 + 13], lns[:, c0 + 15:c0 + 16],
            op0=ALU.subtract, op1=ALU.mult),
           reads=[f"xsl{sl2}", f"lns_mv_{sl}", f"lns_r_{sl}"], writes=[f"xnrow{sl}"])
        bank = 6 + sl
        for fi in range(FI):
            op("t", lambda fi=fi, sl=sl, bank=bank: ENG["t"].transpose(
                pbf[bank][:, fi * 128:(fi + 1) * 128],
                xnrow[:, sl * 1024 + fi * 128: sl * 1024 + (fi + 1) * 128], ident[:, :]),
               reads=[f"xnrow{sl}", "ident"], writes=[f"pb{bank}"])
        def evac_xnt(t=t, bank=bank):
            dst = xnt[:, :].rearrange("p (c n) -> p c n", c=FI)[:, :, t * 128:(t + 1) * 128]
            srcp = pbf[bank][:, 0:1024].rearrange("p (c n) -> p c n", c=FI)
            return ENG["v"].tensor_copy(dst, srcp)
        op("v", evac_xnt, reads=[f"pb{bank}"], writes=[f"xnt{t}"])

    def vproj(j):
        tok0 = 128 * j - 64
        pstart = 64 if j == 0 else 0
        pcnt = 64 if j in (0, 16) else 128
        a0 = max(tok0, 0)
        tiles = sorted({a0 // 128, (a0 + pcnt - 1) // 128})
        xr = [f"xnt{tt}" for tt in tiles]
        bank0 = (j % 2) * 2
        for half in range(2):
            bank = bank0 + half
            for fi in range(FI):
                op("t", lambda j=j, half=half, fi=fi, bank=bank, a0=a0, pcnt=pcnt:
                   ENG["t"].matmul(
                       pb[bank][0:pcnt, :],
                       xnt[:, fi * 2048 + a0: fi * 2048 + a0 + pcnt],
                       wv_sb[:, fi * 1024 + half * 512: fi * 1024 + (half + 1) * 512],
                       start=(fi == 0), stop=(fi == FI - 1)),
                   reads=xr + [f"wv{fi // 2}"], writes=[f"pb{bank}"])
            jc = 0 if j == 16 else j
            op("v", lambda jc=jc, half=half, bank=bank, pstart=pstart, pcnt=pcnt, j=j:
               ENG["v"].tensor_tensor(
                   vtsh[pstart:pstart + pcnt, jc * 1024 + half * 512: jc * 1024 + (half + 1) * 512],
                   pb[bank][0:pcnt, :],
                   bvr[pstart:pstart + pcnt, half * 512:(half + 1) * 512], op=ALU.add),
               reads=[f"pb{bank}", "bvr"], writes=[f"vtsh{j}h{half}"])

    for t in range(NT + 6):
        if t < NT:
            p1_a(t)
        if 0 <= t - 1 < NT:
            p1_b(t - 1)
        if 0 <= t - 2 < NT:
            p1_c(t - 2)
        if PH >= 2 and 0 <= t - 3 <= 16:
            vproj(t - 3)

    if PH >= 2:
        for pi, (wh, dst, bias, nm) in enumerate(
                [(wq_h, qf, bqc, "q")]):
            for fo in range(FI):
                slot = (pi * 8 + fo) % 8
                op("g", lambda wh=wh, fo=fo, slot=slot: ENG["g"].dma_start(
                    wbig[:, slot * 1024:(slot + 1) * 1024], wh[:, fo * 1024:(fo + 1) * 1024]),
                   writes=[f"wslot{slot}"], inc=16)
                banks = [4 + (fo % 2) * 2, 5 + (fo % 2) * 2]
                for tc in range(4):
                    bank = banks[tc % 2]
                    bname = f"pb{bank}"
                    for fi in range(FI):
                        op("t", lambda fo=fo, tc=tc, fi=fi, slot=slot, bank=bank:
                           ENG["t"].matmul(
                               pb[bank][:, :],
                               wbig[:, slot * 1024 + fi * 128: slot * 1024 + (fi + 1) * 128],
                               xnt[:, fi * 2048 + tc * 512: fi * 2048 + (tc + 1) * 512],
                               start=(fi == 0), stop=(fi == FI - 1)),
                           reads=[f"wslot{slot}"] + [f"xnt{tt}" for tt in range(tc * 4, tc * 4 + 4)],
                           writes=[bname])
                    op("s", lambda dst=dst, bias=bias, fo=fo, tc=tc, bank=bank:
                       ENG["s"].activation(
                           dst[:, fo * 2048 + tc * 512: fo * 2048 + (tc + 1) * 512],
                           pb[bank][:, :], AF.Identity, bias=bias[:, fo:fo + 1], scale=1.0),
                       reads=[f"pb{bank}", "bqc" if nm == "q" else "bkc"],
                       writes=[f"{nm}f{fo}_{tc}"])

        for fo in range(FI):
            op("g", lambda fo=fo: ENG["g"].dma_start(
                wbig[:, fo * 1024:(fo + 1) * 1024], wk_h[:, fo * 1024:(fo + 1) * 1024]),
               writes=[f"wslot{fo}"], inc=16)

        for tc in range(4):
            bank = 0 + (tc % 2)
            for fi in range(FI):
                op("t", lambda tc=tc, fi=fi, bank=bank: ENG["t"].matmul(
                    pb[bank][:, :],
                    win_sb[:, fi * 128:(fi + 1) * 128],
                    xnt[:, fi * 2048 + tc * 512: fi * 2048 + (tc + 1) * 512],
                    start=(fi == 0), stop=(fi == FI - 1)),
                   reads=["win_sb"] + [f"xnt{tt}" for tt in range(tc * 4, tc * 4 + 4)],
                   writes=[f"pb{bank}"])
            op("s", lambda tc=tc, bank=bank: ENG["s"].activation(
                s2[:, tc * 512:(tc + 1) * 512], pb[bank][:, :], AF.Identity,
                bias=binp[:, :], scale=1.0),
               reads=[f"pb{bank}", "binp"], writes=[f"s2_{tc // 2}"])
        for jj in range(8):
            for hf in range(2):
                sl = (jj * 2 + hf) % 2
                op("s", lambda jj=jj, hf=hf, sl=sl: ENG["s"].activation(
                    hb[:, sl * 1024:(sl + 1) * 1024], s2[:, hf * 1024:(hf + 1) * 1024], AF.Relu,
                    bias=b1p[:, jj:jj + 1], scale=w1p[:, jj:jj + 1]),
                   reads=[f"s2_{hf}", "w1p", "b1p"], writes=[f"hb{sl}"])
                for q2 in range(2):
                    bank = 2 + (hf * 2 + q2) % 4
                    op("t", lambda jj=jj, hf=hf, sl=sl, q2=q2, bank=bank: ENG["t"].matmul(
                        pb[bank][0:64, :],
                        w2s_sb[:, jj * 64:(jj + 1) * 64],
                        hb[:, sl * 1024 + q2 * 512: sl * 1024 + (q2 + 1) * 512],
                        start=(jj == 0), stop=(jj == 7)),
                       reads=[f"hb{sl}", "w2s_sb"], writes=[f"pb{bank}"])
        for tc in range(4):
            bank = 2 + tc
            op("s", lambda tc=tc, bank=bank: ENG["s"].activation(
                g_sb[:, tc * 512:(tc + 1) * 512], pb[bank][0:64, :], AF.Identity,
                bias=b2v[:, :], scale=1.0),
               reads=[f"pb{bank}", "b2v"], writes=["g"])

    P.barrier()
    esA.close()

    esBkeep = ExitStack()
    afm = esBkeep.enter_context(nc.sbuf_tensor("afm", [128, 16384], BF16))
    esB = ExitStack()
    sbufB = lambda n_, s_, d_: esB.enter_context(nc.sbuf_tensor(n_, s_, d_))
    mms = sbufB("mms", [128, 256], BF16)
    m0s = mms[:, 64:256]
    m15s = mms[:, 0:192]
    svdgsv = sbufB("svdgsv", [33, 1024], BF16)
    svdg = svdgsv[0:4, :]
    svrow = svdgsv[32:33, :]
    emrt = sbufB("emrt", [4, 256], BF16)
    stt = sbufB("stt", [128, 112], F32)
    emrec = sbufB("emrec", [128, 16], BF16)
    wbuf = sbufB("wbuf", [128, 2048], F32)
    ebuf = sbufB("ebuf", [128, 2048], BF16)
    ppb = sbufB("ppb", [128, 2048], BF16)
    ppt = sbufB("ppt", [128, 2048], BF16)

    op("g", lambda: ENG["g"].dma_start(mms[:, :], maskm_h), writes=["mm", "m0", "m15"], inc=16)


    if PH >= 3:
        for half in range(2):
            for j in range(16):
                op("t", lambda j=j, half=half: ENG["t"].matmul(
                    pb[6 + half][0:1, 0:512],
                    ones_b[:, :],
                    vtsh[:, j * 1024 + half * 512: j * 1024 + (half + 1) * 512],
                    start=(j == 0), stop=(j == 15)),
                   reads=[f"vtsh{j}h{half}", "ones_b"] +
                         ([f"vtsh16h{half}"] if j == 0 else []),
                   writes=[f"pb{6 + half}"])
        for half in range(2):
            op("v", lambda half=half: ENG["v"].tensor_copy(
                svrow[:, half * 512:(half + 1) * 512], pb[6 + half][0:1, :]),
               reads=[f"pb{6 + half}"], writes=["svrow"])
        op("v", lambda: ENG["v"].memset(svdg[:, :], 0.0), writes=["svdg"])
        for h in range(H):
            g_, r_ = h // 4, h % 4
            op("g", lambda h=h, g_=g_, r_=r_: ENG["g"].dma_start(
                svdg[r_:r_ + 1, g_ * 256 + r_ * 64: g_ * 256 + (r_ + 1) * 64],
                svrow[0:1, h * 64:(h + 1) * 64]),
               reads=["svrow"], writes=["svdg"], inc=16)

        def qt_params(qt):
            if qt == 0:
                return 192, "m0", m0s, 0, [(0, 64, 0, 64), (64, 128, 1, 0)]
            if qt == NT - 1:
                return 192, "m15", m15s, 1856, [(0, 128, 15, 0), (128, 64, 16, 0)]
            return 256, "mm", mms, 128 * qt - 64, [(0, 128, qt, 0), (128, 128, qt + 1, 0)]

        NTASK = NT * 4

        def stage_scores(k):
            qt, g_ = divmod(k, 4)
            W, mname, mask, wstart, pieces = qt_params(qt)
            pair = (0, 1) if k % 2 == 0 else (2, 3)
            for j in range(4):
                h = 4 * g_ + j
                hc, hbase = h // 2, (h % 2) * 64
                bank = pair[j % 2]
                col = (j // 2) * 256
                op("t", lambda h=h, hc=hc, hbase=hbase, qt=qt, W=W, wstart=wstart,
                       bank=bank, col=col:
                   ENG["t"].matmul(
                       pb[bank][:, col:col + W],
                       qf[hbase:hbase + 64, hc * 2048 + qt * 128: hc * 2048 + (qt + 1) * 128],
                       kf[hbase:hbase + 64, hc * 2048 + wstart: hc * 2048 + wstart + W],
                       start=True, stop=True, tile_position=(hbase, 0)),
                   reads=[f"qf{hc}_{qt // 4}"] +
                         [f"kf{hc}_{tc}" for tc in range(wstart // 512, (wstart + W - 1) // 512 + 1)],
                   writes=[f"pb{bank}"], trows=(hbase, hbase + 64))

        def stage_sm_a(k):
            qt, g_ = divmod(k, 4)
            W, mname, mask, wstart, pieces = qt_params(qt)
            pair = (0, 1) if k % 2 == 0 else (2, 3)
            c0 = 4 * g_
            for j in range(4):
                h = c0 + j
                bank, col = pair[j % 2], (j // 2) * 256
                op("v", lambda h=h, bank=bank, col=col, W=W, mask=mask:
                   ENG["v"].tensor_tensor_reduce(
                       wbuf[:, (h % 8) * 256: (h % 8) * 256 + W],
                       pb[bank][:, col:col + W],
                       mask[:, 0:W], 1.0, 0.0, ALU.mult, ALU.max,
                       accum_out=stt[:, h:h + 1]),
                   reads=[f"pb{bank}", mname], writes=[f"wS{h % 8}", f"mb{h}"])
            op("v", lambda c0=c0: ENG["v"].tensor_scalar(
                stt[:, 16 + c0:16 + c0 + 4], stt[:, c0:c0 + 4], -1.0, None, op0=ALU.mult),
               reads=[f"mb{h}" for h in range(c0, c0 + 4)], writes=[f"negmG{g_}"])

        def stage_sm_b(k):
            qt, g_ = divmod(k, 4)
            W, mname, mask, wstart, pieces = qt_params(qt)
            c0 = 4 * g_
            op("s", lambda c0=c0: ENG["s"].activation(
                stt[:, 48 + c0:48 + c0 + 4], stt[:, 16 + c0:16 + c0 + 4], AF.Exp),
               reads=[f"negmG{g_}"], writes=[f"emG{g_}"])
            for j in range(4):
                h = c0 + j
                op("s", lambda h=h, W=W: ENG["s"].activation(
                    ebuf[:, (h % 8) * 256: (h % 8) * 256 + W], wbuf[:, (h % 8) * 256: (h % 8) * 256 + W], AF.Exp,
                    bias=stt[:, 16 + h:16 + h + 1], scale=1.0,
                    accum_out=stt[:, 32 + h:32 + h + 1]),
                   reads=[f"wS{h % 8}", f"negmG{g_}"], writes=[f"eS{h % 8}", f"sp{h}"])

        def stage_sm_c(k):
            qt, g_ = divmod(k, 4)
            W, mname, mask, wstart, pieces = qt_params(qt)
            c0 = 4 * g_
            op("v", lambda c0=c0, W=W: ENG["v"].tensor_scalar(
                stt[:, 64 + c0:64 + c0 + 4], stt[:, 48 + c0:48 + c0 + 4],
                float(N - W), None, op0=ALU.mult),
               reads=[f"emG{g_}"], writes=[f"ztG{g_}"])
            op("v", lambda c0=c0: ENG["v"].tensor_tensor(
                stt[:, 80 + c0:80 + c0 + 4], stt[:, 64 + c0:64 + c0 + 4],
                stt[:, 32 + c0:32 + c0 + 4], op=ALU.add),
               reads=[f"ztG{g_}"] + [f"sp{h}" for h in range(c0, c0 + 4)],
               writes=[f"zG{g_}"])
            op("v", lambda c0=c0: ENG["v"].reciprocal(
                stt[:, 96 + c0:96 + c0 + 4], stt[:, 80 + c0:80 + c0 + 4]),
               reads=[f"zG{g_}"], writes=[f"rzG{g_}"])
            op("v", lambda c0=c0: ENG["v"].tensor_tensor(
                emrec[:, c0:c0 + 4], stt[:, 48 + c0:48 + c0 + 4],
                stt[:, 96 + c0:96 + c0 + 4], op=ALU.mult),
               reads=[f"emG{g_}", f"rzG{g_}"], writes=[f"emrG{g_}"])
            for j in range(4):
                h = c0 + j
                op("v", lambda h=h, W=W: ENG["v"].tensor_scalar(
                    ppb[:, (h % 8) * 256: (h % 8) * 256 + W], ebuf[:, (h % 8) * 256: (h % 8) * 256 + W],
                    stt[:, 48 + h:48 + h + 1], stt[:, 96 + h:96 + h + 1],
                    op0=ALU.subtract, op1=ALU.mult),
                   reads=[f"eS{h % 8}", f"emG{g_}", f"rzG{g_}"], writes=[f"ppS{h % 8}"])

        def stage_pt(k):
            qt, g_ = divmod(k, 4)
            W, mname, mask, wstart, pieces = qt_params(qt)
            bank = 4
            ev = "v" if qt % 2 == 0 else "s"
            c0 = 4 * g_
            op("t", lambda c0=c0, bank=bank: ENG["t"].transpose(
                pbf[bank][0:4, 0:128], emrec[:, c0:c0 + 4], ident[:, :]),
               reads=[f"emrG{g_}", "ident"], writes=[f"pb{bank}"])
            op(ev, lambda bank=bank, ev=ev, k=k: (ENG["v"].tensor_copy if ev == "v" else ENG["s"].copy)(
                emrt[0:4, (k % 2) * 128:(k % 2) * 128 + 128], pbf[bank][0:4, 0:128]),
               reads=[f"pb{bank}"], writes=[f"emrt{k % 2}"])
            if qt in (0, NT - 1):
                for j in range(4):
                    if qt == 0:
                        zap = pb[bank][0:64, j * 128: j * 128 + 64]
                    else:
                        zap = pb[bank][64:128, j * 128 + 64: j * 128 + 128]
                    op("v", lambda zap=zap: ENG["v"].memset(zap, 0.0),
                       reads=[], writes=[f"pb{bank}"])
            for j in range(4):
                h = c0 + j
                for i, (cc0, cw, vj, pstart) in enumerate(pieces):
                    op("t", lambda h=h, j=j, i=i, cc0=cc0, cw=cw, pstart=pstart, bank=bank:
                       ENG["t"].transpose(
                           pbf[bank][pstart:pstart + cw, j * 256 + i * 128: j * 256 + (i + 1) * 128],
                           ppb[:, (h % 8) * 256 + cc0: (h % 8) * 256 + cc0 + cw],
                           ident[:, :], tile_position=(0, pstart)),
                       reads=[f"ppS{h % 8}", "ident"], writes=[f"pb{bank}"])
            op(ev, lambda bank=bank, k=k, ev=ev: (ENG["v"].tensor_copy if ev == "v" else ENG["s"].copy)(
                ppt[:, (k % 2) * 1024:(k % 2) * 1024 + 1024], pbf[bank][:, 0:1024]),
               reads=[f"pb{bank}"], writes=[f"ppts{k % 2}"])

        def stage_pv(k):
            qt, g_ = divmod(k, 4)
            W, mname, mask, wstart, pieces = qt_params(qt)
            c0 = 4 * g_
            ev = "v" if qt % 2 == 0 else "s"
            for j in range(4):
                h = c0 + j
                cch = 2 * g_ + j // 2
                bankp = 6 + (j % 2)
                op("t", lambda h=h, g_=g_, j=j, bankp=bankp, k=k: ENG["t"].matmul(
                    pb[bankp][0:64, 0:128],
                    svdg[0:4, g_ * 256 + j * 64: g_ * 256 + (j + 1) * 64],
                    emrt[0:4, (k % 2) * 128:(k % 2) * 128 + 128],
                    start=True, stop=False, tile_position=(0, 0)),
                   reads=["svdg", f"emrt{k % 2}"], writes=[f"pb{bankp}"], trows=(0, 4))
                for i, (cc0, cw, vj, pstart) in enumerate(pieces):
                    op("t", lambda h=h, j=j, i=i, cw=cw, vj=vj, pstart=pstart,
                           bankp=bankp, k=k:
                       ENG["t"].matmul(
                           pb[bankp][0:64, 0:128],
                           vtsh[pstart:pstart + cw,
                                (0 if vj == 16 else vj) * 1024 + h * 64:
                                (0 if vj == 16 else vj) * 1024 + (h + 1) * 64],
                           ppt[pstart:pstart + cw,
                               (k % 2) * 1024 + j * 256 + i * 128: (k % 2) * 1024 + j * 256 + (i + 1) * 128],
                           start=False, stop=(i == 1), tile_position=(pstart, 0)),
                       reads=[f"ppts{k % 2}", f"vtsh{vj}h{h // 8}"], writes=[f"pb{bankp}"],
                       trows=(pstart, pstart + cw))
                op(ev, lambda qt=qt, h=h, cch=cch, bankp=bankp, ev=ev:
                   (ENG["v"].tensor_copy if ev == "v" else ENG["s"].copy)(
                       afm[(h % 2) * 64:(h % 2) * 64 + 64,
                           cch * 2048 + qt * 128: cch * 2048 + (qt + 1) * 128],
                       pb[bankp][0:64, 0:128]),
                   reads=[f"pb{bankp}"], writes=[f"afm{qt}_{g_ // 2}"])

        def kproj_group(fo, tc):
            for fi in range(FI):
                op("t", lambda fo=fo, tc=tc, fi=fi: ENG["t"].matmul(
                    pb[5][:, :],
                    wbig[:, fo * 1024 + fi * 128: fo * 1024 + (fi + 1) * 128],
                    xnt[:, fi * 2048 + tc * 512: fi * 2048 + (tc + 1) * 512],
                    start=(fi == 0), stop=(fi == FI - 1)),
                   reads=[f"wslot{fo}"] + [f"xnt{tt}" for tt in range(tc * 4, tc * 4 + 4)],
                   writes=["pb5"])
            op("s", lambda fo=fo, tc=tc: ENG["s"].activation(
                kf[:, fo * 2048 + tc * 512: fo * 2048 + (tc + 1) * 512],
                pb[5][:, :], AF.Identity, bias=bkc[:, fo:fo + 1], scale=1.0),
               reads=["pb5", "bkc"], writes=[f"kf{fo}_{tc}"])

        for fo in range(FI):
            kproj_group(fo, 0)

        for k in range(NTASK + 5):
            if k < NTASK:
                stage_scores(k)
            if 0 <= k - 1 < NTASK:
                stage_sm_a(k - 1)
            if 0 <= k - 2 < NTASK:
                stage_sm_b(k - 2)
            if 0 <= k - 3 < NTASK:
                stage_sm_c(k - 3)
            if 0 <= k - 4 < NTASK:
                stage_pt(k - 4)
            if 0 <= k - 5 < NTASK:
                stage_pv(k - 5)
            if k % 2 == 0 and k < 48:
                kproj_group((k % 16) // 2, 1 + k // 16)
            if k >= 48 and k < 56 and k % 2 == 0:
                i = (k - 48) // 2
                op("g", lambda i=i: ENG["g"].dma_start(
                    wbig[:, i * 2048:(i + 1) * 2048], wo_h[:, i * 2048:(i + 1) * 2048]),
                   writes=[f"wslot{2 * i}", f"wslot{2 * i + 1}"], inc=16)

    P.barrier()
    esB.close()
    qfF = qf[:, :].bitcast(F32)
    kfF = kf[:, :].bitcast(F32)
    xin4 = qfF[:, 0:4096]
    xb = qfF[:, 4096:7168]
    outs = kfF[:, 0:2048]
    bcomb = kfF[:, 2048:3072]

    op("y", lambda: ENG["y"].dma_start(outs[:, 0:1024], bor_h), writes=["outs0"], inc=16)
    op("y", lambda: ENG["y"].dma_start(outs[:, 1024:2048], boutr_h), writes=["outs1"], inc=16)
    op("v", lambda: ENG["v"].tensor_tensor(bcomb[:, :], outs[:, 0:1024], outs[:, 1024:2048],
                                           op=ALU.add),
       reads=["outs0", "outs1"], writes=["bcomb"])

    if PH >= 4:
        def p4_a(t):
            slab, sl2 = t // 2, (t // 2) % 2
            if t % 2 == 0:
                op("g", lambda slab=slab, sl2=sl2: ENG["g"].dma_start(
                    xin4[:, :].rearrange("p (a e) -> p a e", a=4)[:, sl2 * 2:sl2 * 2 + 2, :],
                    xslab_ap(slab * 2)),
                   writes=[f"xsl{sl2}"], inc=16)
            xoff = (sl2 * 2 + (t % 2)) * 1024
            op("v", lambda t=t, xoff=xoff, sl2=sl2: ENG["v"].tensor_tensor(
                xb[:, (t % 3) * 1024:(t % 3 + 1) * 1024],
                xin4[:, xoff:xoff + 1024], bcomb[:, :], op=ALU.add),
               reads=[f"xsl{sl2}", "bcomb"], writes=[f"xb{t % 3}"])

        def p4_b(t):
            for hf in range(2):
                bank = (t % 2) * 2 + hf
                for c in range(FI):
                    op("t", lambda t=t, hf=hf, c=c, bank=bank: ENG["t"].matmul(
                        pb[bank][:, :],
                        afm[:, c * 2048 + t * 128: c * 2048 + (t + 1) * 128],
                        wbig[:, c * 1024 + hf * 512: c * 1024 + (hf + 1) * 512],
                        start=(c == 0), stop=False),
                       reads=[f"afm{t}_{c // 4}", f"wslot{c}"], writes=[f"pb{bank}"])
                op("t", lambda t=t, hf=hf, bank=bank: ENG["t"].matmul(
                    pb[bank][:, :],
                    g_sb[:, t * 128:(t + 1) * 128],
                    wout_sb[:, hf * 512:(hf + 1) * 512],
                    start=False, stop=True),
                   reads=["g", "wout_sb"], writes=[f"pb{bank}"], trows=(0, 64))

        def p4_c(t):
            sl = t % 2
            for hf in range(2):
                bank = (t % 2) * 2 + hf
                op("v", lambda t=t, sl=sl, hf=hf, bank=bank: ENG["v"].tensor_tensor(
                    outs[:, sl * 1024 + hf * 512: sl * 1024 + (hf + 1) * 512],
                    pb[bank][:, :], xb[:, (t % 3) * 1024 + hf * 512: (t % 3) * 1024 + (hf + 1) * 512],
                    op=ALU.add),
                   reads=[f"pb{bank}", f"xb{t % 3}"], writes=[f"outs{sl}"])
            op("y", lambda t=t, sl=sl: ENG["y"].dma_start(
                y_h[t * 128:(t + 1) * 128, :], outs[:, sl * 1024:(sl + 1) * 1024]),
               reads=[f"outs{sl}"], writes=[], inc=16)

        for t in range(NT + 2):
            if 0 <= t - 2 < NT:
                p4_c(t - 2)
            if t < NT:
                p4_a(t)
            if 0 <= t - 1 < NT:
                p4_b(t - 1)

    P.final_waits()

    semh = {}
    for s_ in P.sems:
        semh[s_] = es.enter_context(nc.semaphore(f"sem_{s_}"))
    block = es.enter_context(nc.Block())

    def replay(e, engobj):
        ENG[e] = engobj
        for item in P.stream[e]:
            if item[0] == "wait":
                engobj.wait_ge(semh[item[1]], item[2])
            else:
                instr = item[1]()
                instr.then_inc(semh[item[3]], item[2])

    @block.tensor
    def _(t):
        replay("t", t)

    @block.vector
    def _(v):
        replay("v", v)

    @block.scalar
    def _(s):
        replay("s", s)

    @block.sync
    def _(s):
        replay("y", s)

    @block.gpsimd
    def _(g):
        replay("g", g)

    esBkeep.close()
    esX.close()
    es.close()
    lower_extended_insts(nc)
    return nc


def _host_consts(i):
    f = np.ascontiguousarray
    Wq = np.asarray(i["L0_attn_Wq"], np.float32)
    Wk = np.asarray(i["L0_attn_Wk"], np.float32)
    Wv = np.asarray(i["L0_attn_Wv"], np.float32)
    Wo = np.asarray(i["L0_attn_Wo"], np.float32)
    bq = np.asarray(i["L0_attn_bq"], np.float32)
    bk = np.asarray(i["L0_attn_bk"], np.float32)
    bv = np.asarray(i["L0_attn_bv"], np.float32)
    bo = np.asarray(i["L0_attn_bo"], np.float32)
    Win = np.asarray(i["L0_ffn_Win"], np.float32)
    bin_ = np.asarray(i["L0_ffn_bin_"], np.float32)
    W1 = np.asarray(i["L0_ffn_W1"], np.float32)
    b1 = np.asarray(i["L0_ffn_b1"], np.float32)
    W2 = np.asarray(i["L0_ffn_W2"], np.float32)
    b2 = np.asarray(i["L0_ffn_b2"], np.float32)
    Wout = np.asarray(i["L0_ffn_Wout"], np.float32)
    bout = np.asarray(i["L0_ffn_bout"], np.float32)

    d = {}
    d["wq"] = f(Wq.reshape(8, 128, 8, 128).transpose(1, 2, 0, 3).reshape(128, 8192))
    d["wk"] = f(Wk.reshape(8, 128, 8, 128).transpose(1, 2, 0, 3).reshape(128, 8192))
    d["wv"] = f(Wv.reshape(8, 128, 1024).transpose(1, 0, 2).reshape(128, 8192))
    d["wo"] = f(Wo.reshape(8, 128, 1024).transpose(1, 0, 2).reshape(128, 8192))
    d["bqc"] = f(bq.reshape(8, 128).T)
    d["bkc"] = f(bk.reshape(8, 128).T)
    d["bvr"] = f(np.broadcast_to(bv[None, :], (128, 1024)))
    wind = Win.reshape(8, 128, 64)
    d["win"] = f(np.concatenate([wind, wind], axis=2).transpose(1, 0, 2).reshape(128, 1024))
    d["binp"] = f(np.concatenate([bin_, bin_]).reshape(128, 1))
    d["w1p"] = f(np.concatenate([W1[:, 0::2], W1[:, 1::2]], axis=0))
    d["b1p"] = f(np.concatenate([b1[:, 0::2], b1[:, 1::2]], axis=0))
    w2s = np.zeros((128, 512), np.float32)
    for m in range(64):
        for j in range(8):
            w2s[m, j * 64 + m] = W2[m, 2 * j]
            w2s[64 + m, j * 64 + m] = W2[m, 2 * j + 1]
    d["w2s"] = w2s
    d["b2v"] = f(b2.reshape(64, 1))
    d["wout"] = f(Wout)
    d["bor"] = f(np.broadcast_to(bo[None, :], (128, 1024)))
    d["boutr"] = f(np.broadcast_to(bout[None, :], (128, 1024)))
    r = np.arange(128)[:, None]
    c0 = np.arange(192)[None, :]
    cm = np.arange(256)[None, :]
    d["mask0"] = (np.abs(c0 - r) <= 64).astype(np.float32)
    d["maskm"] = ((cm >= r) & (cm <= r + 128)).astype(np.float32)
    d["mask15"] = ((c0 >= r) & (c0 <= r + 128)).astype(np.float32)
    d["identc"] = np.eye(128, dtype=np.float32)
    d["onesc"] = np.ones((128, 1), np.float32)
    return d


_NC_CACHE = None


def kernel(**inputs):
    global _NC_CACHE
    x = np.ascontiguousarray(np.asarray(inputs["L0_attn_x"], np.float32))
    consts = _host_consts(inputs)
    if _NC_CACHE is None:
        _NC_CACHE = build_nc()
    nc = _NC_CACHE
    in_maps = []
    for c in range(NCORES):
        m = {"x": np.ascontiguousarray(x[c])}
        m.update(consts)
        in_maps.append(m)
    res = run_bass_kernel_spmd(nc, in_maps, core_ids=list(range(NCORES)))
    return np.stack([res.results[c]["y"] for c in range(NCORES)], axis=0)


if __name__ == "__main__":
    import reference as _r
    ins = _r.setup_inputs()
    ins = {k: np.asarray(v) for k, v in ins.items()}
    out = kernel(**ins)
    ref = np.asarray(_r.reference(**{k: v for k, v in ins.items()}))
    err = np.abs(out - ref)
    rv = float(((out - ref) ** 2).mean() / (ref ** 2).mean())
    print("max abs err:", err.max(), "resid var ratio:", rv)
